# version 18
# baseline (speedup 1.0000x reference)
import contextlib
import numpy as np
import ml_dtypes

import concourse.bass as bass
import concourse.mybir as mybir
from concourse.bass_utils import run_bass_kernel_spmd

F32 = mybir.dt.float32
BF16 = mybir.dt.bfloat16
I32 = mybir.dt.int32
U16 = mybir.dt.uint16
AF = mybir.ActivationFunctionType
ALU = mybir.AluOpType
AX = mybir.AxisListType

H = 4096
S = 2048
TOK = 1024
D = 1024
FI = 11008
NHC = H // 128
NDC = D // 128
NIC = FI // 128
NKC = S // 128
EPS = 1e-5
QSCALE = 1.0 / 32.0

ENGINES = ("sync", "vector", "scalar", "tensor", "gpsimd")


def nc_sem_helper(nc, name):
    return nc.semaphore(name)


class _Sem:
    __slots__ = ("h", "c", "name", "is_dma", "waited_max", "drained")

    def __init__(self, h, name, is_dma=False):
        self.h = h
        self.c = 0
        self.name = name
        self.is_dma = is_dma
        self.waited_max = 0
        self.drained = 0


class Buf:
    __slots__ = ("w", "readers", "name")

    def __init__(self, name=""):
        self.w = None
        self.readers = []
        self.name = name


class Eng:
    def __init__(self, name, sem):
        self.name = name
        self.sem = sem
        self.items = []
        self.wm = {}

    def _wait(self, sem, val):
        if sem.is_dma:
            val = sem.c
        if val <= 0 or self.wm.get(sem, 0) >= val:
            return
        self.wm[sem] = val
        if sem.is_dma and val > sem.waited_max:
            sem.waited_max = val
        self.items.append(lambda e, s=sem, v=val: e.wait_ge(s.h, v))

    def _deps(self, reads, writes):
        for b in reads:
            if b.w is not None:
                self._wait(*b.w)
        for b in writes:
            if b.w is not None:
                self._wait(*b.w)
            for (rs, rv) in b.readers:
                self._wait(rs, rv)

    def _stamp(self, reads, writes):
        c = self.sem.c
        for b in writes:
            b.w = (self.sem, c)
            b.readers = []
        for b in reads:
            b.readers.append((self.sem, c))

    def op(self, fn, reads=(), writes=()):
        self._deps(reads, writes)
        self.sem.c += 1
        self.items.append(
            lambda e, f=fn, s=self.sem: f(e).then_inc(s.h, 1))
        self._stamp(reads, writes)

    def group(self, fns, reads=(), writes=()):
        self._deps(reads, writes)
        self.sem.c += 1
        for f in fns[:-1]:
            self.items.append(lambda e, f=f: f(e))
        self.items.append(
            lambda e, f=fns[-1], s=self.sem: f(e).then_inc(s.h, 1))
        self._stamp(reads, writes)


class Prog:
    QUEUE_OWNER = {"ldq": "sync", "stq": "gpsimd", "scq": "scalar"}

    def __init__(self, nc, ctx):
        self.nc = nc
        self.ctx = ctx
        self.e = {}
        self.dsems = {}
        for name in ENGINES:
            h = ctx.enter_context(nc.semaphore(f"sem_{name}"))
            self.e[name] = Eng(name, _Sem(h, name))

    def _dsem(self, qname, stream):
        key = (qname, stream)
        if key not in self.dsems:
            h = self.ctx.enter_context(nc_sem_helper(self.nc, f"ds_{qname}_{stream}"))
            self.dsems[key] = _Sem(h, f"{qname}_{stream}", is_dma=True)
        return self.dsems[key]

    def dma(self, qname, fn, reads=(), writes=(), stream=None):
        sem = self._dsem(qname, stream or qname)
        eng = self.e[self.QUEUE_OWNER[qname]]
        if sem.waited_max > sem.drained:
            eng._wait(sem, sem.waited_max)
            sem.drained = sem.waited_max
        eng._deps(reads, writes)
        sem.c += 16
        eng.items.append(lambda e, f=fn, s=sem: f(e).then_inc(s.h, 16))
        c = sem.c
        for b in writes:
            b.w = (sem, c)
            b.readers = []
        for b in reads:
            b.readers.append((sem, c))

    def barrier(self):
        sems = [e.sem for e in self.e.values()] + list(self.dsems.values())
        for eng in self.e.values():
            for s in sems:
                if s is not eng.sem:
                    eng._wait(s, s.c)


def build_nc(taps=()):
    nc = bass.Bass("TRN2", target_bir_lowering=False)

    def dram(name, shape, dtype, kind):
        return nc.dram_tensor(name, shape, dtype, kind=kind).ap()

    d_xT = dram("xT", [H, S], F32, "ExternalInput")
    d_wqkv = dram("wqkv", [H, 3 * D], F32, "ExternalInput")
    d_bqkv = dram("bqkv_r", [128, 24], F32, "ExternalInput")
    d_wo = dram("wo", [D, H], F32, "ExternalInput")
    d_bo = dram("bo_r", [128, 32], F32, "ExternalInput")
    d_gq = dram("gq", [FI, H], U16, "ExternalInput")
    d_gusz = dram("gusz", [FI, 128], F32, "ExternalInput")
    d_uq = dram("uq", [FI, H], U16, "ExternalInput")

    d_dq = dram("dq", [H, FI], U16, "ExternalInput")
    d_dsdz = dram("dsdz", [H, 172], F32, "ExternalInput")
    d_id = dram("ident", [128, 128], BF16, "ExternalInput")

    d_y = dram("y", [H, TOK], F32, "ExternalOutput")
    d_q = dram("q_scr", [D, TOK], BF16, "Internal")
    d_k = dram("k_scr", [D, S], BF16, "Internal")
    d_v = dram("v_scr", [D, S], BF16, "Internal")
    d_r1 = dram("r1_scr", [H, TOK], F32, "Internal")
    d_ha = dram("hact_scr", [FI, TOK], BF16, "Internal")
    d_at_dbg = dram("at_scr", [D, TOK], BF16, "Internal")

    with contextlib.ExitStack() as ctx:
        P = Prog(nc, ctx)
        sy, ve, sc, te, gp = (P.e[n] for n in ENGINES)

        s_ones = ctx.enter_context(nc.sbuf_tensor("ones", [128, 128], BF16))
        b_ones = Buf("ones")
        s_id = ctx.enter_context(nc.sbuf_tensor("ident_sb", [128, 128], BF16))
        b_id = Buf("ident_sb")
        s_bqkv = ctx.enter_context(nc.sbuf_tensor("bqkv_sb", [128, 24], F32))
        b_bqkv = Buf("bqkv_sb")
        s_bo = ctx.enter_context(nc.sbuf_tensor("bo_sb", [128, 32], F32))
        b_bo = Buf("bo_sb")
        s_zero = ctx.enter_context(nc.sbuf_tensor("zero_sb", [128, 8], F32))
        b_zero = Buf("zero_sb")
        s_eps = ctx.enter_context(nc.sbuf_tensor("eps_sb", [128, 1], F32))
        b_eps = Buf("eps_sb")
        s_ln2 = ctx.enter_context(nc.sbuf_tensor("ln2", [128, 4 * 512], F32))
        b_ln2 = Buf("ln2")

        ve.op(lambda e: e.memset(s_ones[:], 1.0), writes=[b_ones])
        ve.op(lambda e: e.memset(s_zero[:], 0.0), writes=[b_zero])
        ve.op(lambda e: e.memset(s_eps[:], EPS), writes=[b_eps])
        P.dma("ldq", lambda e: e.dma_start(out=s_id[:], in_=d_id), writes=[b_id], stream="cst")
        P.dma("stq", lambda e: e.dma_start(out=s_bqkv[:], in_=d_bqkv), writes=[b_bqkv], stream="cst")
        P.dma("stq", lambda e: e.dma_start(out=s_bo[:], in_=d_bo), writes=[b_bo], stream="cst")

        b_dq_q = Buf("d_q")
        b_dq_k = Buf("d_k")
        b_dq_v = Buf("d_v")
        b_dr1 = Buf("d_r1")
        b_dha = Buf("d_ha")

        b_dy = Buf("d_y")

        with contextlib.ExitStack() as abctx:
            s_xn = abctx.enter_context(nc.sbuf_tensor("xn_all", [128, NHC * S], BF16))
            b_xnc = [Buf(f"xnc{t}") for t in range(8)]

            TC = 256
            NTC = S // TC
            s_xsq = abctx.enter_context(nc.sbuf_tensor("xsq_ping", [128, 2 * TC], BF16))
            b_xsq = [Buf("xsq0"), Buf("xsq1")]
            s_st = abctx.enter_context(nc.sbuf_tensor("ln_st", [128, 2 * TC], F32))
            b_st = Buf("ln_st")
            s_rstd = abctx.enter_context(nc.sbuf_tensor("ln_rstd", [128, 2 * TC], BF16))
            b_rstd = [Buf("rstd0"), Buf("rstd1")]
            s_mr = abctx.enter_context(nc.sbuf_tensor("ln_mr", [128, 2 * TC], BF16))
            b_mr = [Buf("mr0"), Buf("mr1")]
            p_s1 = abctx.enter_context(nc.psum_tensor("pS1", [128, 512], F32))
            p_s2 = abctx.enter_context(nc.psum_tensor("pS2", [128, 512], F32))
            b_ps1 = Buf("pS1")
            b_ps2 = Buf("pS2")

            xT_r = d_xT.rearrange("(c p) t -> p c t", p=128)
            xn_r = s_xn[:].rearrange("p (c t) -> p c t", c=NHC)

            def a_xload(tc):
                P.dma("stq",
                      lambda e, tc=tc: e.dma_start(
                          out=xn_r[:, :, tc * TC:(tc + 1) * TC],
                          in_=xT_r[:, :, tc * TC:(tc + 1) * TC]),
                      writes=[b_xnc[tc]], stream=f"x{tc % 2}")

            a_xload(0)
            a_xload(1)
            for tc in range(NTC):
                pg = tc % 2
                if tc + 2 < NTC:
                    a_xload(tc + 2)
                for hc in range(NHC):
                    cpg = hc % 2
                    xc = s_xn[:, hc * S + tc * TC: hc * S + (tc + 1) * TC]
                    te.group([lambda e, xc=xc, hc=hc: e.matmul(
                        out=p_s1[:, 0:TC], lhsT=s_ones[:], rhs=xc,
                        start=(hc == 0), stop=(hc == NHC - 1))],
                        reads=[b_xnc[tc], b_ones], writes=[b_ps1])
                    sc.op(lambda e, xc=xc, cpg=cpg: e.activation(
                        out=s_xsq[:, cpg * TC:(cpg + 1) * TC], in_=xc,
                        func=AF.Square, bias=0.0, scale=1.0),
                        reads=[b_xnc[tc]], writes=[b_xsq[cpg]])
                    te.group([lambda e, cpg=cpg, hc=hc: e.matmul(
                        out=p_s2[:, 0:TC], lhsT=s_ones[:],
                        rhs=s_xsq[:, cpg * TC:(cpg + 1) * TC],
                        start=(hc == 0), stop=(hc == NHC - 1))],
                        reads=[b_xsq[cpg], b_ones], writes=[b_ps2])
                st_v = s_st[:, pg * TC:(pg + 1) * TC]
                rst = s_rstd[:, pg * TC:(pg + 1) * TC]
                mr = s_mr[:, pg * TC:(pg + 1) * TC]
                sc.op(lambda e, st_v=st_v: e.activation(
                    out=st_v, in_=p_s1[:, 0:TC], func=AF.Square,
                    bias=0.0, scale=1.0 / H),
                    reads=[b_ps1], writes=[b_st])
                ve.op(lambda e, st_v=st_v: e.scalar_tensor_tensor(
                    out=st_v, in0=p_s2[:, 0:TC], scalar=1.0 / H, in1=st_v,
                    op0=ALU.mult, op1=ALU.subtract),
                    reads=[b_ps2, b_st], writes=[b_st])
                sc.op(lambda e, st_v=st_v: e.activation(
                    out=st_v, in_=st_v, func=AF.Sqrt, bias=s_eps[:, 0:1], scale=1.0),
                    reads=[b_st, b_eps], writes=[b_st])
                ve.op(lambda e, st_v=st_v: e.reciprocal(out=st_v, in_=st_v),
                      reads=[b_st], writes=[b_st])
                ve.op(lambda e, st_v=st_v, rst=rst: e.tensor_copy(out=rst, in_=st_v),
                      reads=[b_st], writes=[b_rstd[pg]])
                ve.op(lambda e, st_v=st_v, mr=mr: e.scalar_tensor_tensor(
                    out=mr, in0=p_s1[:, 0:TC], scalar=1.0 / H, in1=st_v,
                    op0=ALU.mult, op1=ALU.mult),
                    reads=[b_ps1, b_st], writes=[b_mr[pg]])
                for hc in range(NHC):
                    xc = s_xn[:, hc * S + tc * TC: hc * S + (tc + 1) * TC]
                    ve.op(lambda e, xc=xc, rst=rst: e.tensor_mul(
                        out=xc, in0=xc, in1=rst),
                        reads=[b_rstd[pg], b_xnc[tc]], writes=[b_xnc[tc]])
                    ve.op(lambda e, xc=xc, mr=mr: e.tensor_sub(
                        out=xc, in0=xc, in1=mr),
                        reads=[b_mr[pg], b_xnc[tc]], writes=[b_xnc[tc]])

            s_wqb = abctx.enter_context(nc.sbuf_tensor("wq_bf", [128, 2 * NHC * 128], BF16))
            b_wqb = [Buf("wqb0"), Buf("wqb1")]
            s_qkvo = abctx.enter_context(nc.sbuf_tensor("qkv_o", [128, 4 * 512], BF16))
            b_qkvo = [Buf(f"qkvo{i}") for i in range(4)]
            p_b = [abctx.enter_context(nc.psum_tensor(f"pB{i}", [128, 512], F32)) for i in range(6)]
            b_pb = [Buf(f"pB{i}") for i in range(6)]

            wqkv_r = d_wqkv.rearrange("(c p) f -> p c f", p=128)
            wqb_r = s_wqb[:].rearrange("p (g c f) -> p g c f", g=2, c=NHC)

            def b_wload(ft):
                fpg = ft % 2
                P.dma("stq", lambda e, ft=ft, fpg=fpg: e.dma_start(
                    out=wqb_r[:, fpg],
                    in_=wqkv_r[:, :, ft * 128:(ft + 1) * 128]),
                    writes=[b_wqb[fpg]], stream=f"wq{fpg}")

            b_wload(0)
            gidx = 0
            for ft in range(24):
                fpg = ft % 2
                if ft + 1 < 24:
                    b_wload(ft + 1)
                ntc = 2 if ft < 8 else 4
                for tcc in range(ntc):
                    bank = gidx % 6
                    gidx += 1
                    fns = []
                    for hc in range(NHC):
                        fns.append(lambda e, fpg=fpg, hc=hc, tcc=tcc, bank=bank: e.matmul(
                            out=p_b[bank][:],
                            lhsT=s_wqb[:, fpg * H + hc * 128:
                                       fpg * H + (hc + 1) * 128],
                            rhs=s_xn[:, hc * S + tcc * 512: hc * S + (tcc + 1) * 512],
                            start=(hc == 0), stop=(hc == NHC - 1)))
                    te.group(fns, reads=[b_wqb[fpg], b_xnc[2 * tcc], b_xnc[2 * tcc + 1],
                                         b_ones], writes=[b_pb[bank]])
                    stg = gidx % 4
                    ob = s_qkvo[:, stg * 512:(stg + 1) * 512]
                    if ft < 8:
                        ve.op(lambda e, bank=bank, ft=ft, ob=ob: e.tensor_scalar(
                            out=ob, in0=p_b[bank][:],
                            scalar1=s_bqkv[:, ft:ft + 1], scalar2=QSCALE,
                            op0=ALU.add, op1=ALU.mult),
                            reads=[b_pb[bank], b_bqkv], writes=[b_qkvo[stg]])
                    else:
                        ve.op(lambda e, bank=bank, ft=ft, ob=ob: e.tensor_scalar(
                            out=ob, in0=p_b[bank][:],
                            scalar1=s_bqkv[:, ft:ft + 1], scalar2=None,
                            op0=ALU.add),
                            reads=[b_pb[bank], b_bqkv], writes=[b_qkvo[stg]])
                    if ft < 8:
                        dst, db, rr = d_q, b_dq_q, ft * 128
                    elif ft < 16:
                        dst, db, rr = d_k, b_dq_k, (ft - 8) * 128
                    else:
                        dst, db, rr = d_v, b_dq_v, (ft - 16) * 128
                    cc = tcc * 512
                    P.dma("scq",
                          lambda e, dst=dst, rr=rr, cc=cc, ob=ob: e.dma_start(
                              out=dst[rr:rr + 128, cc:cc + 512], in_=ob),
                          reads=[b_qkvo[stg]], writes=[db], stream=f"qkvo{stg % 2}")

            P.barrier()

        with contextlib.ExitStack() as cdctx:
            s_at = cdctx.enter_context(nc.sbuf_tensor("attnT", [128, NDC * TOK], BF16))
            b_at = Buf("attnT")

            with contextlib.ExitStack() as cctx:
                s_k = cctx.enter_context(nc.sbuf_tensor("k_sb", [128, NDC * S], BF16))
                b_k = Buf("k_sb")
                s_vf = cctx.enter_context(nc.sbuf_tensor("vf_sb", [128, NDC * S], BF16))
                b_vf = Buf("vf_sb")
                s_vt = cctx.enter_context(nc.sbuf_tensor("vt_sb", [128, NKC * D], BF16))
                b_vt = Buf("vt_sb")
                s_q = cctx.enter_context(nc.sbuf_tensor("q_sb", [128, NDC * TOK], BF16))
                b_q = Buf("q_sb")
                s_ssb = cctx.enter_context(nc.sbuf_tensor("s_sb", [128, 2 * S], F32))
                b_ssb = [Buf("ssb0"), Buf("ssb1")]
                s_p = cctx.enter_context(nc.sbuf_tensor("p_sb", [128, 2 * S], BF16))
                b_p = [Buf("p0"), Buf("p1")]
                s_red = cctx.enter_context(nc.sbuf_tensor("red_sb", [128, 24], F32))
                b_red = [Buf(f"red{i}") for i in range(8)]
                s_pt = cctx.enter_context(nc.sbuf_tensor("pt_sb", [128, NKC * 512], BF16))
                b_pt = Buf("pt_sb")
                p_sc = [cctx.enter_context(nc.psum_tensor(f"pSc{i}", [128, 512], F32)) for i in range(2)]
                b_psc = [Buf("pSc0"), Buf("pSc1")]
                p_tr = [cctx.enter_context(nc.psum_tensor(f"pTr{i}", [128, 1024], BF16)) for i in range(2)]
                b_ptr = [Buf("pTr0"), Buf("pTr1")]
                p_o = [cctx.enter_context(nc.psum_tensor(f"pO{i}", [128, 512], F32)) for i in range(2)]
                b_po = [Buf("pO0"), Buf("pO1")]

                k_r = d_k.rearrange("(c p) t -> p c t", p=128)
                v_r = d_v.rearrange("(c p) t -> p c t", p=128)
                q_r = d_q.rearrange("(c p) t -> p c t", p=128)
                ksb_r = s_k[:].rearrange("p (c t) -> p c t", c=NDC)
                vfsb_r = s_vf[:].rearrange("p (c t) -> p c t", c=NDC)
                qsb_r = s_q[:].rearrange("p (c t) -> p c t", c=NDC)

                ve.op(lambda e: e.memset(s_red[:], 0.0), writes=list(b_red))
                P.dma("ldq", lambda e: e.dma_start(out=ksb_r, in_=k_r),
                      reads=[b_dq_k], writes=[b_k], stream="kld")
                P.dma("ldq", lambda e: e.dma_start(out=vfsb_r, in_=v_r),
                      reads=[b_dq_v], writes=[b_vf], stream="vld")
                P.dma("ldq", lambda e: e.dma_start(out=qsb_r, in_=q_r),
                      reads=[b_dq_q], writes=[b_q], stream="qld")

                for kc in range(NKC):
                    pg = kc % 2
                    fns = []
                    for dc in range(NDC):
                        fns.append(lambda e, kc=kc, dc=dc, pg=pg: e.transpose(
                            out=p_tr[pg][:, dc * 128:(dc + 1) * 128],
                            in_=s_vf[:, dc * S + kc * 128: dc * S + (kc + 1) * 128],
                            identity=s_id[:]))
                    te.group(fns, reads=[b_vf, b_id], writes=[b_ptr[pg]])
                    sc.op(lambda e, kc=kc, pg=pg: e.activation(
                        out=s_vt[:, kc * D:(kc + 1) * D], in_=p_tr[pg][:],
                        func=AF.Identity, bias=0.0, scale=1.0),
                        reads=[b_ptr[pg]], writes=[b_vt])

                for qg in range(2):
                    for qi in range(4):
                        qt = qg * 4 + qi
                        pg = qt % 2
                        for kcol in range(4):
                            kb = kcol % 2
                            fns = []
                            for dc in range(NDC):
                                fns.append(lambda e, qt=qt, kcol=kcol, dc=dc, kb=kb: e.matmul(
                                    out=p_sc[kb][:],
                                    lhsT=s_q[:, dc * TOK + qt * 128: dc * TOK + (qt + 1) * 128],
                                    rhs=s_k[:, dc * S + kcol * 512: dc * S + (kcol + 1) * 512],
                                    start=(dc == 0), stop=(dc == NDC - 1)))
                            te.group(fns, reads=[b_q, b_k], writes=[b_psc[kb]])
                            ve.op(lambda e, kcol=kcol, pg=pg, kb=kb: e.tensor_copy(
                                out=s_ssb[:, pg * S + kcol * 512: pg * S + (kcol + 1) * 512],
                                in_=p_sc[kb][:]),
                                reads=[b_psc[kb]], writes=[b_ssb[pg]])
                        srow = s_ssb[:, pg * S:(pg + 1) * S]
                        prow = s_p[:, pg * S:(pg + 1) * S]
                        ve.op(lambda e, srow=srow, qt=qt: e.reduce_max(
                            out=s_red[:, qt: qt + 1], in_=srow,
                            axis=AX.X, negate=True),
                            reads=[b_ssb[pg]], writes=[b_red[qt]])
                        sc.op(lambda e, srow=srow, prow=prow, qt=qt: e.activation(
                            out=prow, in_=srow, func=AF.Exp,
                            bias=s_red[:, qt: qt + 1], scale=1.0,
                            accum_out=s_red[:, 8 + qt: 9 + qt]),
                            reads=[b_ssb[pg], b_red[qt]], writes=[b_p[pg], b_red[qt]])
                        ve.op(lambda e, qt=qt: e.reciprocal(
                            out=s_red[:, 16 + qt: 17 + qt],
                            in_=s_red[:, 8 + qt: 9 + qt]),
                            reads=[b_red[qt]], writes=[b_red[qt]])
                        ve.op(lambda e, prow=prow, qt=qt: e.tensor_scalar(
                            out=prow, in0=prow,
                            scalar1=s_red[:, 16 + qt: 17 + qt], scalar2=None,
                            op0=ALU.mult),
                            reads=[b_red[qt], b_p[pg]], writes=[b_p[pg]])
                        for half in range(2):
                            fns = []
                            for j in range(8):
                                kc = half * 8 + j
                                fns.append(lambda e, kc=kc, j=j, pg=pg, half=half: e.transpose(
                                    out=p_tr[half][:, j * 128:(j + 1) * 128],
                                    in_=s_p[:, pg * S + kc * 128: pg * S + (kc + 1) * 128],
                                    identity=s_id[:]))
                            te.group(fns, reads=[b_p[pg], b_id], writes=[b_ptr[half]])
                            sc.op(lambda e, half=half, qi=qi: e.activation(
                                out=s_pt[:].rearrange("p (c q) -> p c q", c=NKC)
                                [:, half * 8:(half + 1) * 8, qi * 128:(qi + 1) * 128],
                                in_=p_tr[half][:].rearrange("p (c q) -> p c q", c=8),
                                func=AF.Identity, bias=0.0, scale=1.0),
                                reads=[b_ptr[half]], writes=[b_pt])
                    for dt in range(NDC):
                        pg = dt % 2
                        fns = []
                        for kc in range(NKC):
                            fns.append(lambda e, dt=dt, kc=kc, pg=pg: e.matmul(
                                out=p_o[pg][:],
                                lhsT=s_vt[:, kc * D + dt * 128: kc * D + (dt + 1) * 128],
                                rhs=s_pt[:, kc * 512:(kc + 1) * 512],
                                start=(kc == 0), stop=(kc == NKC - 1)))
                        te.group(fns, reads=[b_vt, b_pt], writes=[b_po[pg]])
                        sc.op(lambda e, dt=dt, qg=qg, pg=pg: e.activation(
                            out=s_at[:, dt * TOK + qg * 512: dt * TOK + (qg + 1) * 512],
                            in_=p_o[pg][:], func=AF.Identity, bias=0.0, scale=1.0),
                            reads=[b_po[pg]], writes=[b_at])

                P.barrier()

            if "at_scr" in taps:
                b_atdbg = Buf("at_dbg")
                for dc_ in range(NDC):
                    P.dma("stq", lambda e, dc_=dc_: e.dma_start(
                        out=d_at_dbg[dc_ * 128:(dc_ + 1) * 128, :],
                        in_=s_at[:, dc_ * TOK:(dc_ + 1) * TOK]),
                        reads=[b_at], writes=[b_atdbg], stream="atdbg")

            with contextlib.ExitStack() as dctx:
                s_woc = dctx.enter_context(nc.sbuf_tensor("wo_col", [128, 4 * NDC * 128], F32))
                b_woc = [Buf(f"woc{i}") for i in range(4)]
                s_wob = dctx.enter_context(nc.sbuf_tensor("wo_bf", [128, 4 * NDC * 128], BF16))
                b_wob = [Buf(f"wob{i}") for i in range(4)]
                s_xo = dctx.enter_context(nc.sbuf_tensor("xo_sb", [128, 4 * 512], F32))
                b_xo = [Buf(f"xo{i}") for i in range(4)]
                s_r1c = dctx.enter_context(nc.sbuf_tensor("r1c_sb", [128, 4 * 512], F32))
                b_r1c = [Buf(f"r1c{i}") for i in range(4)]
                s_r1b = dctx.enter_context(nc.sbuf_tensor("r1b_sb", [128, 4 * 512], BF16))
                b_r1b = [Buf(f"r1b{i}") for i in range(4)]
                s_r1q = dctx.enter_context(nc.sbuf_tensor("r1q_sb", [128, 4 * 512], BF16))
                b_r1q = [Buf(f"r1q{i}") for i in range(4)]
                s_l2t = dctx.enter_context(nc.sbuf_tensor("ln2_t", [128, 2 * 512], F32))
                b_l2t = Buf("ln2_t")
                p_w = [dctx.enter_context(nc.psum_tensor(f"pW{i}", [128, 512], F32)) for i in range(4)]
                b_pw = [Buf(f"pW{i}") for i in range(4)]
                p_t1 = [dctx.enter_context(nc.psum_tensor(f"pT1_{i}", [128, 512], F32)) for i in range(2)]
                p_t2 = [dctx.enter_context(nc.psum_tensor(f"pT2_{i}", [128, 512], F32)) for i in range(2)]
                b_pt1 = [Buf("pT1_0"), Buf("pT1_1")]
                b_pt2 = [Buf("pT2_0"), Buf("pT2_1")]

                wo_r = d_wo.rearrange("(c p) h -> p c h", p=128)
                woc_r = s_woc[:].rearrange("p (g c h) -> p g c h", g=4, c=NDC)

                stats_pend = []

                def d_stats(hc, tcc, sl):
                    te.group([lambda e, tcc=tcc, hc=hc, sl=sl: e.matmul(
                        out=p_t1[tcc][:], lhsT=s_ones[:],
                        rhs=s_r1b[:, sl * 512:(sl + 1) * 512],
                        start=(hc == 0), stop=(hc == NHC - 1))],
                        reads=[b_r1b[sl], b_ones], writes=[b_pt1[tcc]])
                    te.group([lambda e, tcc=tcc, hc=hc, sl=sl: e.matmul(
                        out=p_t2[tcc][:], lhsT=s_ones[:],
                        rhs=s_r1q[:, sl * 512:(sl + 1) * 512],
                        start=(hc == 0), stop=(hc == NHC - 1))],
                        reads=[b_r1q[sl], b_ones], writes=[b_pt2[tcc]])

                def d_wload(hc):
                    pg = hc % 4
                    P.dma("ldq",
                          lambda e, hc=hc, pg=pg: e.dma_start(
                              out=woc_r[:, pg], in_=wo_r[:, :, hc * 128:(hc + 1) * 128]),
                          writes=[b_woc[pg]], stream=f"wo{hc % 2}")
                    ve.op(lambda e, pg=pg: e.tensor_copy(
                        out=s_wob[:, pg * 1024:(pg + 1) * 1024],
                        in_=s_woc[:, pg * 1024:(pg + 1) * 1024]),
                        reads=[b_woc[pg]], writes=[b_wob[pg]])

                d_wload(0)
                d_wload(1)
                for hc in range(NHC):
                    pg = hc % 4
                    if hc + 2 < NHC:
                        d_wload(hc + 2)
                    for tcc in range(2):
                        sl = (hc * 2 + tcc) % 4
                        fns = []
                        for dc in range(NDC):
                            fns.append(lambda e, pg=pg, dc=dc, tcc=tcc, sl=sl: e.matmul(
                                out=p_w[sl][:],
                                lhsT=s_wob[:, pg * 1024 + dc * 128: pg * 1024 + (dc + 1) * 128],
                                rhs=s_at[:, dc * TOK + tcc * 512: dc * TOK + (tcc + 1) * 512],
                                start=(dc == 0), stop=(dc == NDC - 1)))
                        te.group(fns, reads=[b_wob[pg], b_at], writes=[b_pw[sl]])
                        P.dma("ldq",
                              lambda e, hc=hc, tcc=tcc, sl=sl: e.dma_start(
                                  out=s_xo[:, sl * 512:(sl + 1) * 512],
                                  in_=d_xT[hc * 128:(hc + 1) * 128, tcc * 512:(tcc + 1) * 512]),
                              writes=[b_xo[sl]], stream=f"xo{sl % 2}")
                        r1c = s_r1c[:, sl * 512:(sl + 1) * 512]
                        ve.op(lambda e, tcc=tcc, hc=hc, r1c=r1c, sl=sl: e.tensor_scalar(
                            out=r1c, in0=p_w[sl][:], scalar1=s_bo[:, hc:hc + 1],
                            scalar2=None, op0=ALU.add),
                            reads=[b_pw[sl], b_bo], writes=[b_r1c[sl]])
                        ve.op(lambda e, r1c=r1c, sl=sl: e.tensor_add(
                            out=r1c, in0=r1c, in1=s_xo[:, sl * 512:(sl + 1) * 512]),
                            reads=[b_xo[sl], b_r1c[sl]], writes=[b_r1c[sl]])
                        P.dma("stq",
                              lambda e, hc=hc, tcc=tcc, r1c=r1c: e.dma_start(
                                  out=d_r1[hc * 128:(hc + 1) * 128, tcc * 512:(tcc + 1) * 512],
                                  in_=r1c),
                              reads=[b_r1c[sl]], writes=[b_dr1], stream=f"r1st{tcc}")
                        ve.op(lambda e, r1c=r1c, sl=sl: e.tensor_copy(
                            out=s_r1b[:, sl * 512:(sl + 1) * 512], in_=r1c),
                            reads=[b_r1c[sl]], writes=[b_r1b[sl]])
                        sc.op(lambda e, r1c=r1c, sl=sl: e.activation(
                            out=s_r1q[:, sl * 512:(sl + 1) * 512], in_=r1c,
                            func=AF.Square, bias=0.0, scale=1.0),
                            reads=[b_r1c[sl]], writes=[b_r1q[sl]])
                        stats_pend.append((hc, tcc, sl))
                        if len(stats_pend) > 2:
                            d_stats(*stats_pend.pop(0))

                for _st_args in stats_pend:
                    d_stats(*_st_args)
                stats_pend = []

                for tcc in range(2):
                    msq = s_l2t[:, tcc * 512:(tcc + 1) * 512]
                    rst2 = s_ln2[:, tcc * 512:(tcc + 1) * 512]
                    mr2 = s_ln2[:, (2 + tcc) * 512:(3 + tcc) * 512]
                    sc.op(lambda e, tcc=tcc, msq=msq: e.activation(
                        out=msq, in_=p_t1[tcc][:], func=AF.Square, bias=0.0, scale=1.0 / H),
                        reads=[b_pt1[tcc]], writes=[b_l2t])
                    ve.op(lambda e, tcc=tcc, msq=msq: e.scalar_tensor_tensor(
                        out=msq, in0=p_t2[tcc][:], scalar=1.0 / H, in1=msq,
                        op0=ALU.mult, op1=ALU.subtract),
                        reads=[b_pt2[tcc], b_l2t], writes=[b_l2t])
                    sc.op(lambda e, msq=msq: e.activation(
                        out=msq, in_=msq, func=AF.Sqrt, bias=s_eps[:, 0:1], scale=1.0),
                        reads=[b_l2t, b_eps], writes=[b_l2t])
                    ve.op(lambda e, msq=msq, rst2=rst2: e.reciprocal(out=rst2, in_=msq),
                          reads=[b_l2t], writes=[b_ln2])
                    ve.op(lambda e, tcc=tcc, rst2=rst2, mr2=mr2: e.scalar_tensor_tensor(
                        out=mr2, in0=p_t1[tcc][:], scalar=1.0 / H, in1=rst2,
                        op0=ALU.mult, op1=ALU.mult),
                        reads=[b_pt1[tcc], b_ln2], writes=[b_ln2])

                P.barrier()

        with contextlib.ExitStack() as ectx:
            s_r1r = ectx.enter_context(nc.sbuf_tensor("r1r", [128, 4 * 512], F32))
            b_r1r = [Buf(f"r1r{i}") for i in range(4)]

            with contextlib.ExitStack() as exctx:
                s_xn2 = exctx.enter_context(nc.sbuf_tensor("xn2", [128, NHC * TOK], BF16))
                b_xn2t = [Buf("xn2t0"), Buf("xn2t1")]

                s_gq = exctx.enter_context(nc.sbuf_tensor("gq_sb", [128, 2 * H], U16))
                s_uq = exctx.enter_context(nc.sbuf_tensor("uq_sb", [128, 2 * H], U16))
                b_gq = [Buf("gq0"), Buf("gq1")]
                b_uq = [Buf("uq0"), Buf("uq1")]
                s_gsz = exctx.enter_context(nc.sbuf_tensor("gsz_sb", [128, 2 * 128], F32))
                b_gsz = [Buf("gsz0"), Buf("gsz1")]
                s_wg = exctx.enter_context(nc.sbuf_tensor("wg_deq", [128, H], BF16))
                s_wu = exctx.enter_context(nc.sbuf_tensor("wu_deq", [128, H], BF16))
                b_wgh = [Buf("wgA"), Buf("wgB")]
                b_wuh = [Buf("wuA"), Buf("wuB")]
                s_wgt = exctx.enter_context(nc.sbuf_tensor("wgT", [128, 2 * H], BF16))
                s_wut = exctx.enter_context(nc.sbuf_tensor("wuT", [128, 2 * H], BF16))
                b_wgt = [Buf("wgT0"), Buf("wgT1")]
                b_wut = [Buf("wuT0"), Buf("wuT1")]
                s_sg = exctx.enter_context(nc.sbuf_tensor("sg_sb", [128, TOK], F32))
                b_sg = Buf("sg_sb")
                s_su = exctx.enter_context(nc.sbuf_tensor("su_sb", [128, TOK], F32))
                b_su = Buf("su_sb")
                s_hac = exctx.enter_context(nc.sbuf_tensor("hact_sb", [128, 2 * TOK], BF16))
                b_hac = [Buf("hac0"), Buf("hac1")]
                p_g = [exctx.enter_context(nc.psum_tensor(f"pG{i}", [128, 512], F32)) for i in range(2)]
                p_u = [exctx.enter_context(nc.psum_tensor(f"pU{i}", [128, 512], F32)) for i in range(2)]
                b_pg = [Buf("pG0"), Buf("pG1")]
                b_pu = [Buf("pU0"), Buf("pU1")]
                p_tre = [exctx.enter_context(nc.psum_tensor(f"pTe{i}", [128, 1024], BF16)) for i in range(2)]
                b_ptre = [Buf("pTe0"), Buf("pTe1")]

                def e_loads(ic):
                    pg = ic % 2
                    P.dma("ldq", lambda e, ic=ic, pg=pg: e.dma_start(
                        out=s_gq[:, pg * H:(pg + 1) * H],
                        in_=d_gq[ic * 128:(ic + 1) * 128, :]),
                        writes=[b_gq[pg]], stream=f"gq{pg}")
                    P.dma("ldq", lambda e, ic=ic, pg=pg: e.dma_start(
                        out=s_uq[:, pg * H:(pg + 1) * H],
                        in_=d_uq[ic * 128:(ic + 1) * 128, :]),
                        writes=[b_uq[pg]], stream=f"uq{pg}")
                    P.dma("ldq", lambda e, ic=ic, pg=pg: e.dma_start(
                        out=s_gsz[:, pg * 128:(pg + 1) * 128],
                        in_=d_gusz[ic * 128:(ic + 1) * 128, :]),
                        writes=[b_gsz[pg]], stream=f"gsz{pg}")

                def e_dequant(ic):
                    pg = ic % 2
                    for half in range(2):
                        fns = []
                        for g in range(half * 16, half * 16 + 16):
                            fns.append(lambda e, pg=pg, g=g: e.tensor_scalar(
                                out=s_wg[:, g * 128:(g + 1) * 128],
                                in0=s_gq[:, pg * H + g * 128: pg * H + (g + 1) * 128],
                                scalar1=s_gsz[:, pg * 128 + 32 + g: pg * 128 + 33 + g],
                                scalar2=s_gsz[:, pg * 128 + g: pg * 128 + g + 1],
                                op0=ALU.subtract, op1=ALU.mult))
                        ve.group(fns, reads=[b_gq[pg], b_gsz[pg]], writes=[b_wgh[half]])
                    for half in range(2):
                        fns = []
                        for g in range(half * 16, half * 16 + 16):
                            fns.append(lambda e, pg=pg, g=g: e.tensor_scalar(
                                out=s_wu[:, g * 128:(g + 1) * 128],
                                in0=s_uq[:, pg * H + g * 128: pg * H + (g + 1) * 128],
                                scalar1=s_gsz[:, pg * 128 + 96 + g: pg * 128 + 97 + g],
                                scalar2=s_gsz[:, pg * 128 + 64 + g: pg * 128 + 65 + g],
                                op0=ALU.subtract, op1=ALU.mult))
                        ve.group(fns, reads=[b_uq[pg], b_gsz[pg]], writes=[b_wuh[half]])

                def e_trans_grp(ic, wsel, grp):
                    pg = ic % 2
                    s_wd, b_wd = (s_wg, b_wgh[grp // 2]) if wsel == 0 else (s_wu, b_wuh[grp // 2])
                    s_wt, b_wt = (s_wgt, b_wgt) if wsel == 0 else (s_wut, b_wut)
                    tpg = (wsel * 4 + grp) % 2
                    fns = []
                    for j in range(8):
                        g = grp * 8 + j
                        fns.append(lambda e, g=g, j=j, tpg=tpg, s_wd=s_wd: e.transpose(
                            out=p_tre[tpg][:, j * 128:(j + 1) * 128],
                            in_=s_wd[:, g * 128:(g + 1) * 128],
                            identity=s_id[:]))
                    te.group(fns, reads=[b_wd, b_id], writes=[b_ptre[tpg]])
                    sc.op(lambda e, grp=grp, tpg=tpg, pg=pg, s_wt=s_wt: e.activation(
                        out=s_wt[:, pg * H + grp * 1024: pg * H + (grp + 1) * 1024],
                        in_=p_tre[tpg][:], func=AF.Identity, bias=0.0, scale=1.0),
                        reads=[b_ptre[tpg]], writes=[b_wt[pg]])

                def e_mms(ic, interleave):
                    pg = ic % 2
                    step = 0
                    for tcc in range(2):
                        for w in range(2):
                            pp, bb, wt, bwt = ((p_g, b_pg, s_wgt, b_wgt) if w == 0
                                               else (p_u, b_pu, s_wut, b_wut))
                            fns = []
                            for hcc in range(NHC):
                                fns.append(lambda e, pg=pg, hcc=hcc, tcc=tcc, pp=pp, wt=wt: e.matmul(
                                    out=pp[tcc][:],
                                    lhsT=wt[:, pg * H + hcc * 128: pg * H + (hcc + 1) * 128],
                                    rhs=s_xn2[:, hcc * TOK + tcc * 512: hcc * TOK + (tcc + 1) * 512],
                                    start=(hcc == 0), stop=(hcc == NHC - 1)))
                            te.group(fns, reads=[bwt[pg], b_xn2t[tcc]], writes=[bb[tcc]])
                            if step == 0:
                                for (iw, ig) in interleave:
                                    e_trans_grp(ic + 1, iw, ig)
                            step += 1
                        sgc = s_sg[:, tcc * 512:(tcc + 1) * 512]
                        suc = s_su[:, tcc * 512:(tcc + 1) * 512]
                        sc.op(lambda e, tcc=tcc, sgc=sgc: e.activation(
                            out=sgc, in_=p_g[tcc][:], func=AF.Silu, bias=0.0, scale=1.0),
                            reads=[b_pg[tcc]], writes=[b_sg])
                        sc.op(lambda e, tcc=tcc, suc=suc: e.activation(
                            out=suc, in_=p_u[tcc][:], func=AF.Identity, bias=0.0, scale=1.0),
                            reads=[b_pu[tcc]], writes=[b_su])
                        gp.op(lambda e, tcc=tcc, pg=pg, sgc=sgc, suc=suc: e.tensor_mul(
                            out=s_hac[:, pg * TOK + tcc * 512: pg * TOK + (tcc + 1) * 512],
                            in0=suc, in1=sgc),
                            reads=[b_sg, b_su], writes=[b_hac[pg]])
                    P.dma("stq", lambda e, ic=ic, pg=pg: e.dma_start(
                        out=d_ha[ic * 128:(ic + 1) * 128, :],
                        in_=s_hac[:, pg * TOK:(pg + 1) * TOK]),
                        reads=[b_hac[pg]], writes=[b_dha], stream=f"hast{pg}")

                ALLGRPS = [(w_, g_) for w_ in range(2) for g_ in range(4)]
                e_loads(0)
                e_dequant(0)
                for (w_, g_) in ALLGRPS:
                    e_trans_grp(0, w_, g_)

                for tcc in range(2):
                    for hc in range(NHC):
                        idx = tcc * NHC + hc
                        pg = idx % 4
                        eng_ = ve if (idx % 3) else gp
                        rr = s_r1r[:, pg * 512:(pg + 1) * 512]
                        P.dma("ldq",
                              lambda e, hc=hc, tcc=tcc, rr=rr: e.dma_start(
                                  out=rr, in_=d_r1[hc * 128:(hc + 1) * 128,
                                                  tcc * 512:(tcc + 1) * 512]),
                              reads=[b_dr1], writes=[b_r1r[pg]], stream=f"r1ld{pg % 2}")
                        rst2 = s_ln2[:, tcc * 512:(tcc + 1) * 512]
                        mr2 = s_ln2[:, (2 + tcc) * 512:(3 + tcc) * 512]
                        eng_.op(lambda e, rr=rr, rst2=rst2: e.tensor_mul(
                            out=rr, in0=rr, in1=rst2),
                            reads=[b_ln2, b_r1r[pg]], writes=[b_r1r[pg]])
                        eng_.op(lambda e, rr=rr, mr2=mr2, hc=hc, tcc=tcc: e.tensor_sub(
                            out=s_xn2[:, hc * TOK + tcc * 512: hc * TOK + (tcc + 1) * 512],
                            in0=rr, in1=mr2),
                            reads=[b_r1r[pg], b_ln2], writes=[b_xn2t[tcc]])


                for ic in range(NIC):
                    if ic + 1 < NIC:
                        e_loads(ic + 1)
                        e_dequant(ic + 1)
                        e_mms(ic, ALLGRPS)
                    else:
                        e_mms(ic, [])

                P.barrier()

            with contextlib.ExitStack() as fctx:
                s_ha = fctx.enter_context(nc.sbuf_tensor("ha_sb", [128, NIC * 512], BF16))
                b_ha = Buf("ha_sb")
                s_dqs = fctx.enter_context(nc.sbuf_tensor("dq_sb", [128, 2 * 2816], U16))
                b_dqs = [Buf("dqs0"), Buf("dqs1")]
                FQ_START = (0, 22, 43, 65)
                FQ_CNT = (22, 21, 22, 21)
                s_dsz = fctx.enter_context(nc.sbuf_tensor("dsz_sb", [128, 2 * 172], F32))
                b_dsz = [Buf("dsz0"), Buf("dsz1")]
                s_wdd = fctx.enter_context(nc.sbuf_tensor("wd_deq", [128, FI], BF16))
                b_wddq = [Buf(f"wdd{i}") for i in range(4)]
                s_wdt = fctx.enter_context(nc.sbuf_tensor("wdT", [128, 2 * FI], BF16))
                b_wdt = [Buf("wdT0"), Buf("wdT1")]
                s_r1f = fctx.enter_context(nc.sbuf_tensor("r1f", [128, 2 * 512], F32))
                b_r1f = [Buf("r1f0"), Buf("r1f1")]
                s_yo = fctx.enter_context(nc.sbuf_tensor("y_sb", [128, 2 * 512], F32))
                b_yo = [Buf("yo0"), Buf("yo1")]
                s_yt = fctx.enter_context(nc.sbuf_tensor("yt_sb", [128, 2 * 512], F32))
                b_yt = [Buf("yt0"), Buf("yt1")]
                p_d = [fctx.enter_context(nc.psum_tensor(f"pD{i}", [128, 512], F32)) for i in range(2)]
                b_pd = [Buf("pD0"), Buf("pD1")]
                p_trf = [fctx.enter_context(nc.psum_tensor(f"pTf{i}", [128, 1024], BF16)) for i in range(2)]
                b_ptrf = [Buf("pTf0"), Buf("pTf1")]

                FTR_SCHED = {1: (0, 1), 2: (2, 3), 3: (4, 5), 4: (6, 7),
                             5: (8, 9), 6: (10,)}
                ha_r = d_ha.rearrange("(c p) t -> p c t", p=128)
                hasb_r = s_ha[:].rearrange("p (c t) -> p c t", c=NIC)

                def f1_loads(hc):
                    pg = hc % 2
                    P.dma("ldq", lambda e, hc=hc, pg=pg: e.dma_start(
                        out=s_dsz[:, pg * 172:(pg + 1) * 172],
                        in_=d_dsdz[hc * 128:(hc + 1) * 128, :]),
                        writes=[b_dsz[pg]], stream=f"dsz{pg}")

                def f1_deq(hc, fq):
                    pg = hc % 2
                    qb = fq % 2
                    c0, cn = FQ_START[fq] * 128, FQ_CNT[fq] * 128
                    P.dma("ldq", lambda e, hc=hc, qb=qb, c0=c0, cn=cn: e.dma_start(
                        out=s_dqs[:, qb * 2816: qb * 2816 + cn],
                        in_=d_dq[hc * 128:(hc + 1) * 128, c0:c0 + cn]),
                        writes=[b_dqs[qb]], stream=f"dq{qb}")
                    fns = []
                    for j in range(FQ_CNT[fq]):
                        ic = FQ_START[fq] + j
                        fns.append(lambda e, pg=pg, qb=qb, j=j, ic=ic: e.tensor_scalar(
                            out=s_wdd[:, ic * 128:(ic + 1) * 128],
                            in0=s_dqs[:, qb * 2816 + j * 128: qb * 2816 + (j + 1) * 128],
                            scalar1=s_dsz[:, pg * 172 + 86 + ic: pg * 172 + 87 + ic],
                            scalar2=s_dsz[:, pg * 172 + ic: pg * 172 + ic + 1],
                            op0=ALU.subtract, op1=ALU.mult))
                    ve.group(fns, reads=[b_dqs[qb], b_dsz[pg]], writes=[b_wddq[fq]])

                def f1_trans_grp(hc, grp):
                    wpg = hc % 2
                    tpg = grp % 2
                    cnt = min(8, NIC - grp * 8)
                    lo = grp * 8
                    hi = lo + cnt - 1
                    qr = [q for q in range(4)
                          if not (FQ_START[q] + FQ_CNT[q] - 1 < lo or FQ_START[q] > hi)]
                    fns = []
                    for j in range(cnt):
                        fns.append(lambda e, grp=grp, j=j, tpg=tpg: e.transpose(
                            out=p_trf[tpg][:, j * 128:(j + 1) * 128],
                            in_=s_wdd[:, (grp * 8 + j) * 128:(grp * 8 + j + 1) * 128],
                            identity=s_id[:]))
                    te.group(fns, reads=[b_wddq[q] for q in qr] + [b_id],
                             writes=[b_ptrf[tpg]])
                    sc.op(lambda e, wpg=wpg, grp=grp, cnt=cnt, tpg=tpg: e.activation(
                        out=s_wdt[:, wpg * FI + grp * 1024: wpg * FI + grp * 1024 + cnt * 128],
                        in_=p_trf[tpg][:, 0:cnt * 128],
                        func=AF.Identity, bias=0.0, scale=1.0),
                        reads=[b_ptrf[tpg]], writes=[b_wdt[wpg]])

                def f_mms(hc, tc0, trs_hc=None, deq_hc=None):
                    pg = hc % 2
                    wpg = hc % 2
                    for sub in range(11):
                        cnt = min(8, NIC - sub * 8)
                        fns = []
                        for j in range(cnt):
                            ic = sub * 8 + j
                            fns.append(lambda e, ic=ic, pg=pg, wpg=wpg, sub=sub: e.matmul(
                                out=p_d[pg][:],
                                lhsT=s_wdt[:, wpg * FI + ic * 128: wpg * FI + (ic + 1) * 128],
                                rhs=s_ha[:, ic * 512:(ic + 1) * 512],
                                start=(ic == 0), stop=(ic == NIC - 1)))
                        te.group(fns, reads=[b_wdt[wpg], b_ha], writes=[b_pd[pg]])
                        if trs_hc is not None:
                            for g_ in FTR_SCHED.get(sub, ()):
                                f1_trans_grp(trs_hc, g_)
                        if deq_hc is not None and sub in (2, 4, 6, 8):
                            f1_deq(deq_hc, (sub - 2) // 2)

                    P.dma("ldq", lambda e, hc=hc, tc0=tc0, pg=pg: e.dma_start(
                        out=s_r1f[:, pg * 512:(pg + 1) * 512],
                        in_=d_r1[hc * 128:(hc + 1) * 128, tc0:tc0 + 512]),
                        reads=[b_dr1], writes=[b_r1f[pg]], stream=f"r1f{pg}")
                    sc.op(lambda e, pg=pg: e.activation(
                        out=s_yt[:, pg * 512:(pg + 1) * 512], in_=p_d[pg][:],
                        func=AF.Identity, bias=0.0, scale=1.0),
                        reads=[b_pd[pg]], writes=[b_yt[pg]])
                    gp.op(lambda e, pg=pg: e.tensor_add(
                        out=s_yo[:, pg * 512:(pg + 1) * 512],
                        in0=s_yt[:, pg * 512:(pg + 1) * 512],
                        in1=s_r1f[:, pg * 512:(pg + 1) * 512]),
                        reads=[b_yt[pg], b_r1f[pg]], writes=[b_yo[pg]])
                    P.dma("stq", lambda e, hc=hc, tc0=tc0, pg=pg: e.dma_start(
                        out=d_y[hc * 128:(hc + 1) * 128, tc0:tc0 + 512],
                        in_=s_yo[:, pg * 512:(pg + 1) * 512]),
                        reads=[b_yo[pg]], writes=[b_dy], stream=f"yst{pg}")

                P.dma("ldq", lambda e: e.dma_start(
                    out=hasb_r, in_=ha_r[:, :, 0:512]),
                    reads=[b_dha], writes=[b_ha], stream="hald")
                _PRO_DEQ = {2: 0, 5: 1, 8: 2, 10: 3}

                def f_prologue(hc0):
                    f1_loads(hc0)
                    for fq in range(4):
                        f1_deq(hc0, fq)
                    f1_loads(hc0 + 1)
                    for grp in range(11):
                        f1_trans_grp(hc0, grp)
                        if grp in _PRO_DEQ:
                            f1_deq(hc0 + 1, _PRO_DEQ[grp])

                f_prologue(0)
                for hc in range(NHC):
                    if hc + 2 < NHC:
                        f1_loads(hc + 2)
                    f_mms(hc, 0,
                          trs_hc=hc + 1 if hc + 1 < NHC else None,
                          deq_hc=hc + 2 if hc + 2 < NHC else None)
                P.barrier()
                P.dma("ldq", lambda e: e.dma_start(
                    out=hasb_r, in_=ha_r[:, :, 512:1024]),
                    reads=[b_dha], writes=[b_ha], stream="hald")
                f_prologue(0)
                for hc in range(NHC):
                    if hc + 2 < NHC:
                        f1_loads(hc + 2)
                    f_mms(hc, 512,
                          trs_hc=hc + 1 if hc + 1 < NHC else None,
                          deq_hc=hc + 2 if hc + 2 < NHC else None)

                for (qn, _st), s_ in list(P.dsems.items()):
                    if qn in ("stq", "scq"):
                        gp._wait(s_, s_.c)
                P.barrier()

        scr_map = {"q_scr": d_q, "k_scr": d_k, "v_scr": d_v,
                   "r1_scr": d_r1, "hact_scr": d_ha, "at_scr": d_at_dbg}
        for name in taps:
            dsrc = scr_map[name]
            d_tap = nc.dram_tensor("tap_" + name, list(dsrc.shape), F32,
                                   kind="ExternalOutput").ap()
            P.dma("stq", lambda e, d_tap=d_tap, dsrc=dsrc: e.dma_start(
                out=d_tap, in_=dsrc), stream="taps")
        if taps:
            for (qn, _st), s_ in list(P.dsems.items()):
                if qn == "stq":
                    gp._wait(s_, s_.c)

        with nc.Block() as block:
            @block.sync
            def _(e):
                for f in P.e["sync"].items:
                    f(e)

            @block.vector
            def _(e):
                for f in P.e["vector"].items:
                    f(e)

            @block.scalar
            def _(e):
                for f in P.e["scalar"].items:
                    f(e)

            @block.tensor
            def _(e):
                for f in P.e["tensor"].items:
                    f(e)

            @block.gpsimd
            def _(e):
                for f in P.e["gpsimd"].items:
                    f(e)

    return nc


def _lowu16(a):
    a = np.ascontiguousarray(np.asarray(a, dtype="<i4"))
    return np.ascontiguousarray(a.view(np.uint16)[:, 0::2])


def _prep_inputs(inputs):
    x = np.asarray(inputs["L0_attn_x"], dtype=np.float32)
    bqkv = np.asarray(inputs["L0_attn_bqkv"], dtype=np.float32)
    bo = np.asarray(inputs["L0_attn_bo"], dtype=np.float32)
    com = {
        "wqkv": np.ascontiguousarray(np.asarray(inputs["L0_attn_Wqkv"], dtype=np.float32)),
        "bqkv_r": np.ascontiguousarray(bqkv.reshape(24, 128).T),
        "wo": np.ascontiguousarray(np.asarray(inputs["L0_attn_Wo"], dtype=np.float32)),
        "bo_r": np.ascontiguousarray(bo.reshape(32, 128).T),
        "gq": _lowu16(inputs["L0_ffn_gate_qweight"]),
        "uq": _lowu16(inputs["L0_ffn_up_qweight"]),
        "dq": _lowu16(inputs["L0_ffn_down_qweight"]),
        "gusz": np.ascontiguousarray(np.concatenate([
            np.asarray(inputs["L0_ffn_gate_scales"], dtype=np.float32),
            np.asarray(inputs["L0_ffn_gate_zeros"], dtype=np.float32),
            np.asarray(inputs["L0_ffn_up_scales"], dtype=np.float32),
            np.asarray(inputs["L0_ffn_up_zeros"], dtype=np.float32)], axis=1)),
        "dsdz": np.ascontiguousarray(np.concatenate([
            np.asarray(inputs["L0_ffn_down_scales"], dtype=np.float32),
            np.asarray(inputs["L0_ffn_down_zeros"], dtype=np.float32)], axis=1)),
        "ident": np.eye(128, dtype=ml_dtypes.bfloat16),
    }
    in_maps = []
    for c in range(8):
        b, hh = c // 2, c % 2
        own = x[b, hh * TOK:(hh + 1) * TOK].T
        other = x[b, (1 - hh) * TOK:(2 - hh) * TOK].T
        m = dict(com)
        m["xT"] = np.ascontiguousarray(np.concatenate([own, other], axis=1))
        in_maps.append(m)
    return in_maps


def kernel(**inputs):
    in_maps = _prep_inputs(inputs)
    nc = build_nc()
    res = run_bass_kernel_spmd(nc, in_maps, core_ids=list(range(8)))
    out = np.empty((4, 2048, 4096), dtype=np.float32)
    for c in range(8):
        b, hh = c // 2, c % 2
        y = np.asarray(res.results[c]["y"], dtype=np.float32)
        out[b, hh * TOK:(hh + 1) * TOK, :] = y.T
    return out



# version 19 (final)
# speedup vs baseline: 1.1591x; 1.1591x over previous
import contextlib
import numpy as np
import ml_dtypes

import concourse.bass as bass
import concourse.mybir as mybir
from concourse.bass_utils import run_bass_kernel_spmd

F32 = mybir.dt.float32
BF16 = mybir.dt.bfloat16
I32 = mybir.dt.int32
U16 = mybir.dt.uint16
AF = mybir.ActivationFunctionType
ALU = mybir.AluOpType
AX = mybir.AxisListType

H = 4096
S = 2048
TOK = 1024
D = 1024
FI = 11008
NHC = H // 128
NDC = D // 128
NIC = FI // 128
NKC = S // 128
EPS = 1e-5
QSCALE = 1.0 / 32.0

ENGINES = ("sync", "vector", "scalar", "tensor", "gpsimd")


def nc_sem_helper(nc, name):
    return nc.semaphore(name)


class _Sem:
    __slots__ = ("h", "c", "name", "is_dma", "waited_max", "drained")

    def __init__(self, h, name, is_dma=False):
        self.h = h
        self.c = 0
        self.name = name
        self.is_dma = is_dma
        self.waited_max = 0
        self.drained = 0


class Buf:
    __slots__ = ("w", "readers", "name")

    def __init__(self, name=""):
        self.w = None
        self.readers = []
        self.name = name


class Eng:
    def __init__(self, name, sem):
        self.name = name
        self.sem = sem
        self.items = []
        self.wm = {}

    def _wait(self, sem, val):
        if sem.is_dma:
            val = sem.c
        if val <= 0 or self.wm.get(sem, 0) >= val:
            return
        self.wm[sem] = val
        if sem.is_dma and val > sem.waited_max:
            sem.waited_max = val
        self.items.append(lambda e, s=sem, v=val: e.wait_ge(s.h, v))

    def _deps(self, reads, writes):
        for b in reads:
            if b.w is not None:
                self._wait(*b.w)
        for b in writes:
            if b.w is not None:
                self._wait(*b.w)
            for (rs, rv) in b.readers:
                self._wait(rs, rv)

    def _stamp(self, reads, writes):
        c = self.sem.c
        for b in writes:
            b.w = (self.sem, c)
            b.readers = []
        for b in reads:
            b.readers.append((self.sem, c))

    def op(self, fn, reads=(), writes=()):
        self._deps(reads, writes)
        self.sem.c += 1
        self.items.append(
            lambda e, f=fn, s=self.sem: f(e).then_inc(s.h, 1))
        self._stamp(reads, writes)

    def group(self, fns, reads=(), writes=()):
        self._deps(reads, writes)
        self.sem.c += 1
        for f in fns[:-1]:
            self.items.append(lambda e, f=f: f(e))
        self.items.append(
            lambda e, f=fns[-1], s=self.sem: f(e).then_inc(s.h, 1))
        self._stamp(reads, writes)


class Prog:
    QUEUE_OWNER = {"ldq": "sync", "stq": "gpsimd", "scq": "scalar"}

    def __init__(self, nc, ctx):
        self.nc = nc
        self.ctx = ctx
        self.e = {}
        self.dsems = {}
        for name in ENGINES:
            h = ctx.enter_context(nc.semaphore(f"sem_{name}"))
            self.e[name] = Eng(name, _Sem(h, name))

    def _dsem(self, qname, stream):
        key = (qname, stream)
        if key not in self.dsems:
            h = self.ctx.enter_context(nc_sem_helper(self.nc, f"ds_{qname}_{stream}"))
            self.dsems[key] = _Sem(h, f"{qname}_{stream}", is_dma=True)
        return self.dsems[key]

    def dma(self, qname, fn, reads=(), writes=(), stream=None):
        sem = self._dsem(qname, stream or qname)
        eng = self.e[self.QUEUE_OWNER[qname]]
        if sem.waited_max > sem.drained:
            eng._wait(sem, sem.waited_max)
            sem.drained = sem.waited_max
        eng._deps(reads, writes)
        sem.c += 16
        eng.items.append(lambda e, f=fn, s=sem: f(e).then_inc(s.h, 16))
        c = sem.c
        for b in writes:
            b.w = (sem, c)
            b.readers = []
        for b in reads:
            b.readers.append((sem, c))

    def barrier(self):
        sems = [e.sem for e in self.e.values()] + list(self.dsems.values())
        for eng in self.e.values():
            for s in sems:
                if s is not eng.sem:
                    eng._wait(s, s.c)


def build_nc(taps=()):
    nc = bass.Bass("TRN2", target_bir_lowering=False)

    def dram(name, shape, dtype, kind):
        return nc.dram_tensor(name, shape, dtype, kind=kind).ap()

    d_xT = dram("xT", [H, S], F32, "ExternalInput")
    d_wqkv = dram("wqkv", [H, 3 * D], F32, "ExternalInput")
    d_bqkv = dram("bqkv_r", [128, 24], F32, "ExternalInput")
    d_wo = dram("wo", [D, H], F32, "ExternalInput")
    d_bo = dram("bo_r", [128, 32], F32, "ExternalInput")
    d_gq = dram("gq", [FI, H], U16, "ExternalInput")
    d_gusz = dram("gusz", [FI, 128], F32, "ExternalInput")
    d_uq = dram("uq", [FI, H], U16, "ExternalInput")

    d_dq = dram("dq", [H, FI], U16, "ExternalInput")
    d_dsdz = dram("dsdz", [H, 172], F32, "ExternalInput")
    d_id = dram("ident", [128, 128], BF16, "ExternalInput")

    d_y = dram("y", [H, TOK], F32, "ExternalOutput")
    d_q = dram("q_scr", [D, TOK], BF16, "Internal")
    d_k = dram("k_scr", [D, S], BF16, "Internal")
    d_v = dram("v_scr", [D, S], BF16, "Internal")
    d_r1 = dram("r1_scr", [H, TOK], F32, "Internal")
    d_ha = dram("hact_scr", [FI, TOK], BF16, "Internal")
    d_at_dbg = dram("at_scr", [D, TOK], BF16, "Internal")

    with contextlib.ExitStack() as ctx:
        P = Prog(nc, ctx)
        sy, ve, sc, te, gp = (P.e[n] for n in ENGINES)

        s_ones = ctx.enter_context(nc.sbuf_tensor("ones", [128, 128], BF16))
        b_ones = Buf("ones")
        s_id = ctx.enter_context(nc.sbuf_tensor("ident_sb", [128, 128], BF16))
        b_id = Buf("ident_sb")
        s_bqkv = ctx.enter_context(nc.sbuf_tensor("bqkv_sb", [128, 24], F32))
        b_bqkv = Buf("bqkv_sb")
        s_bo = ctx.enter_context(nc.sbuf_tensor("bo_sb", [128, 32], F32))
        b_bo = Buf("bo_sb")
        s_zero = ctx.enter_context(nc.sbuf_tensor("zero_sb", [128, 8], F32))
        b_zero = Buf("zero_sb")
        s_eps = ctx.enter_context(nc.sbuf_tensor("eps_sb", [128, 1], F32))
        b_eps = Buf("eps_sb")
        s_ln2 = ctx.enter_context(nc.sbuf_tensor("ln2", [128, 4 * 512], F32))
        b_ln2 = Buf("ln2")

        ve.op(lambda e: e.memset(s_ones[:], 1.0), writes=[b_ones])
        ve.op(lambda e: e.memset(s_zero[:], 0.0), writes=[b_zero])
        ve.op(lambda e: e.memset(s_eps[:], EPS), writes=[b_eps])
        P.dma("ldq", lambda e: e.dma_start(out=s_id[:], in_=d_id), writes=[b_id], stream="cst")
        P.dma("stq", lambda e: e.dma_start(out=s_bqkv[:], in_=d_bqkv), writes=[b_bqkv], stream="cst")
        P.dma("stq", lambda e: e.dma_start(out=s_bo[:], in_=d_bo), writes=[b_bo], stream="cst")

        b_dq_q = Buf("d_q")
        b_dq_k = Buf("d_k")
        b_dq_v = Buf("d_v")
        b_dr1 = Buf("d_r1")
        b_dha = Buf("d_ha")

        b_dy = Buf("d_y")

        with contextlib.ExitStack() as abctx:
            s_xn = abctx.enter_context(nc.sbuf_tensor("xn_all", [128, NHC * S], BF16))
            b_xnc = [Buf(f"xnc{t}") for t in range(8)]

            TC = 256
            NTC = S // TC
            s_xsq = abctx.enter_context(nc.sbuf_tensor("xsq_ping", [128, 2 * TC], BF16))
            b_xsq = [Buf("xsq0"), Buf("xsq1")]
            s_st = abctx.enter_context(nc.sbuf_tensor("ln_st", [128, 2 * TC], F32))
            b_st = Buf("ln_st")
            s_rstd = abctx.enter_context(nc.sbuf_tensor("ln_rstd", [128, 2 * TC], BF16))
            b_rstd = [Buf("rstd0"), Buf("rstd1")]
            s_mr = abctx.enter_context(nc.sbuf_tensor("ln_mr", [128, 2 * TC], BF16))
            b_mr = [Buf("mr0"), Buf("mr1")]
            p_s1 = abctx.enter_context(nc.psum_tensor("pS1", [128, 512], F32))
            p_s2 = abctx.enter_context(nc.psum_tensor("pS2", [128, 512], F32))
            b_ps1 = Buf("pS1")
            b_ps2 = Buf("pS2")

            xT_r = d_xT.rearrange("(c p) t -> p c t", p=128)
            xn_r = s_xn[:].rearrange("p (c t) -> p c t", c=NHC)

            def a_xload(tc):
                P.dma("stq",
                      lambda e, tc=tc: e.dma_start(
                          out=xn_r[:, :, tc * TC:(tc + 1) * TC],
                          in_=xT_r[:, :, tc * TC:(tc + 1) * TC]),
                      writes=[b_xnc[tc]], stream=f"x{tc % 2}")

            a_xload(0)
            a_xload(1)
            for tc in range(NTC):
                pg = tc % 2
                if tc + 2 < NTC:
                    a_xload(tc + 2)
                for hc in range(NHC):
                    cpg = hc % 2
                    xc = s_xn[:, hc * S + tc * TC: hc * S + (tc + 1) * TC]
                    te.group([lambda e, xc=xc, hc=hc: e.matmul(
                        out=p_s1[:, 0:TC], lhsT=s_ones[:], rhs=xc,
                        start=(hc == 0), stop=(hc == NHC - 1))],
                        reads=[b_xnc[tc], b_ones], writes=[b_ps1])
                    sc.op(lambda e, xc=xc, cpg=cpg: e.activation(
                        out=s_xsq[:, cpg * TC:(cpg + 1) * TC], in_=xc,
                        func=AF.Square, bias=0.0, scale=1.0),
                        reads=[b_xnc[tc]], writes=[b_xsq[cpg]])
                    te.group([lambda e, cpg=cpg, hc=hc: e.matmul(
                        out=p_s2[:, 0:TC], lhsT=s_ones[:],
                        rhs=s_xsq[:, cpg * TC:(cpg + 1) * TC],
                        start=(hc == 0), stop=(hc == NHC - 1))],
                        reads=[b_xsq[cpg], b_ones], writes=[b_ps2])
                st_v = s_st[:, pg * TC:(pg + 1) * TC]
                rst = s_rstd[:, pg * TC:(pg + 1) * TC]
                mr = s_mr[:, pg * TC:(pg + 1) * TC]
                sc.op(lambda e, st_v=st_v: e.activation(
                    out=st_v, in_=p_s1[:, 0:TC], func=AF.Square,
                    bias=0.0, scale=1.0 / H),
                    reads=[b_ps1], writes=[b_st])
                ve.op(lambda e, st_v=st_v: e.scalar_tensor_tensor(
                    out=st_v, in0=p_s2[:, 0:TC], scalar=1.0 / H, in1=st_v,
                    op0=ALU.mult, op1=ALU.subtract),
                    reads=[b_ps2, b_st], writes=[b_st])
                sc.op(lambda e, st_v=st_v: e.activation(
                    out=st_v, in_=st_v, func=AF.Sqrt, bias=s_eps[:, 0:1], scale=1.0),
                    reads=[b_st, b_eps], writes=[b_st])
                ve.op(lambda e, st_v=st_v: e.reciprocal(out=st_v, in_=st_v),
                      reads=[b_st], writes=[b_st])
                ve.op(lambda e, st_v=st_v, rst=rst: e.tensor_copy(out=rst, in_=st_v),
                      reads=[b_st], writes=[b_rstd[pg]])
                ve.op(lambda e, st_v=st_v, mr=mr: e.scalar_tensor_tensor(
                    out=mr, in0=p_s1[:, 0:TC], scalar=1.0 / H, in1=st_v,
                    op0=ALU.mult, op1=ALU.mult),
                    reads=[b_ps1, b_st], writes=[b_mr[pg]])
                for hc in range(NHC):
                    xc = s_xn[:, hc * S + tc * TC: hc * S + (tc + 1) * TC]
                    ve.op(lambda e, xc=xc, rst=rst: e.tensor_mul(
                        out=xc, in0=xc, in1=rst),
                        reads=[b_rstd[pg], b_xnc[tc]], writes=[b_xnc[tc]])
                    ve.op(lambda e, xc=xc, mr=mr: e.tensor_sub(
                        out=xc, in0=xc, in1=mr),
                        reads=[b_mr[pg], b_xnc[tc]], writes=[b_xnc[tc]])

            s_wqb = abctx.enter_context(nc.sbuf_tensor("wq_bf", [128, 2 * NHC * 128], BF16))
            b_wqb = [Buf("wqb0"), Buf("wqb1")]
            s_qkvo = abctx.enter_context(nc.sbuf_tensor("qkv_o", [128, 4 * 512], BF16))
            b_qkvo = [Buf(f"qkvo{i}") for i in range(4)]
            p_b = [abctx.enter_context(nc.psum_tensor(f"pB{i}", [128, 512], F32)) for i in range(6)]
            b_pb = [Buf(f"pB{i}") for i in range(6)]

            wqkv_r = d_wqkv.rearrange("(c p) f -> p c f", p=128)
            wqb_r = s_wqb[:].rearrange("p (g c f) -> p g c f", g=2, c=NHC)

            def b_wload(ft):
                fpg = ft % 2
                P.dma("stq", lambda e, ft=ft, fpg=fpg: e.dma_start(
                    out=wqb_r[:, fpg],
                    in_=wqkv_r[:, :, ft * 128:(ft + 1) * 128]),
                    writes=[b_wqb[fpg]], stream=f"wq{fpg}")

            b_wload(0)
            gidx = 0
            for ft in range(24):
                fpg = ft % 2
                if ft + 1 < 24:
                    b_wload(ft + 1)
                ntc = 2 if ft < 8 else 4
                for tcc in range(ntc):
                    bank = gidx % 6
                    gidx += 1
                    fns = []
                    for hc in range(NHC):
                        fns.append(lambda e, fpg=fpg, hc=hc, tcc=tcc, bank=bank: e.matmul(
                            out=p_b[bank][:],
                            lhsT=s_wqb[:, fpg * H + hc * 128:
                                       fpg * H + (hc + 1) * 128],
                            rhs=s_xn[:, hc * S + tcc * 512: hc * S + (tcc + 1) * 512],
                            start=(hc == 0), stop=(hc == NHC - 1)))
                    te.group(fns, reads=[b_wqb[fpg], b_xnc[2 * tcc], b_xnc[2 * tcc + 1],
                                         b_ones], writes=[b_pb[bank]])
                    stg = gidx % 4
                    ob = s_qkvo[:, stg * 512:(stg + 1) * 512]
                    if ft < 8:
                        ve.op(lambda e, bank=bank, ft=ft, ob=ob: e.tensor_scalar(
                            out=ob, in0=p_b[bank][:],
                            scalar1=s_bqkv[:, ft:ft + 1], scalar2=QSCALE,
                            op0=ALU.add, op1=ALU.mult),
                            reads=[b_pb[bank], b_bqkv], writes=[b_qkvo[stg]])
                    else:
                        ve.op(lambda e, bank=bank, ft=ft, ob=ob: e.tensor_scalar(
                            out=ob, in0=p_b[bank][:],
                            scalar1=s_bqkv[:, ft:ft + 1], scalar2=None,
                            op0=ALU.add),
                            reads=[b_pb[bank], b_bqkv], writes=[b_qkvo[stg]])
                    if ft < 8:
                        dst, db, rr = d_q, b_dq_q, ft * 128
                    elif ft < 16:
                        dst, db, rr = d_k, b_dq_k, (ft - 8) * 128
                    else:
                        dst, db, rr = d_v, b_dq_v, (ft - 16) * 128
                    cc = tcc * 512
                    P.dma("scq",
                          lambda e, dst=dst, rr=rr, cc=cc, ob=ob: e.dma_start(
                              out=dst[rr:rr + 128, cc:cc + 512], in_=ob),
                          reads=[b_qkvo[stg]], writes=[db], stream=f"qkvo{stg % 2}")

            P.barrier()

        with contextlib.ExitStack() as cdctx:
            s_at = cdctx.enter_context(nc.sbuf_tensor("attnT", [128, NDC * TOK], BF16))
            b_at = Buf("attnT")

            with contextlib.ExitStack() as cctx:
                s_k = cctx.enter_context(nc.sbuf_tensor("k_sb", [128, NDC * S], BF16))
                b_k = Buf("k_sb")
                s_vf = cctx.enter_context(nc.sbuf_tensor("vf_sb", [128, NDC * S], BF16))
                b_vf = Buf("vf_sb")
                s_vt = cctx.enter_context(nc.sbuf_tensor("vt_sb", [128, NKC * D], BF16))
                b_vt = Buf("vt_sb")
                s_q = cctx.enter_context(nc.sbuf_tensor("q_sb", [128, NDC * TOK], BF16))
                b_q = Buf("q_sb")
                s_ssb = cctx.enter_context(nc.sbuf_tensor("s_sb", [128, 2 * S], F32))
                b_ssb = [Buf("ssb0"), Buf("ssb1")]
                s_p = cctx.enter_context(nc.sbuf_tensor("p_sb", [128, 2 * S], BF16))
                b_p = [Buf("p0"), Buf("p1")]
                s_red = cctx.enter_context(nc.sbuf_tensor("red_sb", [128, 24], F32))
                b_red = [Buf(f"red{i}") for i in range(8)]
                s_pt = cctx.enter_context(nc.sbuf_tensor("pt_sb", [128, NKC * 512], BF16))
                b_pt = Buf("pt_sb")
                p_sc = [cctx.enter_context(nc.psum_tensor(f"pSc{i}", [128, 512], F32)) for i in range(2)]
                b_psc = [Buf("pSc0"), Buf("pSc1")]
                p_tr = [cctx.enter_context(nc.psum_tensor(f"pTr{i}", [128, 1024], BF16)) for i in range(2)]
                b_ptr = [Buf("pTr0"), Buf("pTr1")]
                p_o = [cctx.enter_context(nc.psum_tensor(f"pO{i}", [128, 512], F32)) for i in range(2)]
                b_po = [Buf("pO0"), Buf("pO1")]

                k_r = d_k.rearrange("(c p) t -> p c t", p=128)
                v_r = d_v.rearrange("(c p) t -> p c t", p=128)
                q_r = d_q.rearrange("(c p) t -> p c t", p=128)
                ksb_r = s_k[:].rearrange("p (c t) -> p c t", c=NDC)
                vfsb_r = s_vf[:].rearrange("p (c t) -> p c t", c=NDC)
                qsb_r = s_q[:].rearrange("p (c t) -> p c t", c=NDC)

                ve.op(lambda e: e.memset(s_red[:], 0.0), writes=list(b_red))
                P.dma("ldq", lambda e: e.dma_start(out=ksb_r, in_=k_r),
                      reads=[b_dq_k], writes=[b_k], stream="kld")
                P.dma("ldq", lambda e: e.dma_start(out=vfsb_r, in_=v_r),
                      reads=[b_dq_v], writes=[b_vf], stream="vld")
                P.dma("ldq", lambda e: e.dma_start(out=qsb_r, in_=q_r),
                      reads=[b_dq_q], writes=[b_q], stream="qld")

                for kc in range(NKC):
                    pg = kc % 2
                    fns = []
                    for dc in range(NDC):
                        fns.append(lambda e, kc=kc, dc=dc, pg=pg: e.transpose(
                            out=p_tr[pg][:, dc * 128:(dc + 1) * 128],
                            in_=s_vf[:, dc * S + kc * 128: dc * S + (kc + 1) * 128],
                            identity=s_id[:]))
                    te.group(fns, reads=[b_vf, b_id], writes=[b_ptr[pg]])
                    sc.op(lambda e, kc=kc, pg=pg: e.activation(
                        out=s_vt[:, kc * D:(kc + 1) * D], in_=p_tr[pg][:],
                        func=AF.Identity, bias=0.0, scale=1.0),
                        reads=[b_ptr[pg]], writes=[b_vt])

                for qg in range(2):
                    for qi in range(4):
                        qt = qg * 4 + qi
                        pg = qt % 2
                        for kcol in range(4):
                            kb = kcol % 2
                            fns = []
                            for dc in range(NDC):
                                fns.append(lambda e, qt=qt, kcol=kcol, dc=dc, kb=kb: e.matmul(
                                    out=p_sc[kb][:],
                                    lhsT=s_q[:, dc * TOK + qt * 128: dc * TOK + (qt + 1) * 128],
                                    rhs=s_k[:, dc * S + kcol * 512: dc * S + (kcol + 1) * 512],
                                    start=(dc == 0), stop=(dc == NDC - 1)))
                            te.group(fns, reads=[b_q, b_k], writes=[b_psc[kb]])
                            ve.op(lambda e, kcol=kcol, pg=pg, kb=kb: e.tensor_copy(
                                out=s_ssb[:, pg * S + kcol * 512: pg * S + (kcol + 1) * 512],
                                in_=p_sc[kb][:]),
                                reads=[b_psc[kb]], writes=[b_ssb[pg]])
                        srow = s_ssb[:, pg * S:(pg + 1) * S]
                        prow = s_p[:, pg * S:(pg + 1) * S]
                        ve.op(lambda e, srow=srow, qt=qt: e.reduce_max(
                            out=s_red[:, qt: qt + 1], in_=srow,
                            axis=AX.X, negate=True),
                            reads=[b_ssb[pg]], writes=[b_red[qt]])
                        sc.op(lambda e, srow=srow, prow=prow, qt=qt: e.activation(
                            out=prow, in_=srow, func=AF.Exp,
                            bias=s_red[:, qt: qt + 1], scale=1.0,
                            accum_out=s_red[:, 8 + qt: 9 + qt]),
                            reads=[b_ssb[pg], b_red[qt]], writes=[b_p[pg], b_red[qt]])
                        ve.op(lambda e, qt=qt: e.reciprocal(
                            out=s_red[:, 16 + qt: 17 + qt],
                            in_=s_red[:, 8 + qt: 9 + qt]),
                            reads=[b_red[qt]], writes=[b_red[qt]])
                        ve.op(lambda e, prow=prow, qt=qt: e.tensor_scalar(
                            out=prow, in0=prow,
                            scalar1=s_red[:, 16 + qt: 17 + qt], scalar2=None,
                            op0=ALU.mult),
                            reads=[b_red[qt], b_p[pg]], writes=[b_p[pg]])
                        for half in range(2):
                            fns = []
                            for j in range(8):
                                kc = half * 8 + j
                                fns.append(lambda e, kc=kc, j=j, pg=pg, half=half: e.transpose(
                                    out=p_tr[half][:, j * 128:(j + 1) * 128],
                                    in_=s_p[:, pg * S + kc * 128: pg * S + (kc + 1) * 128],
                                    identity=s_id[:]))
                            te.group(fns, reads=[b_p[pg], b_id], writes=[b_ptr[half]])
                            sc.op(lambda e, half=half, qi=qi: e.activation(
                                out=s_pt[:].rearrange("p (c q) -> p c q", c=NKC)
                                [:, half * 8:(half + 1) * 8, qi * 128:(qi + 1) * 128],
                                in_=p_tr[half][:].rearrange("p (c q) -> p c q", c=8),
                                func=AF.Identity, bias=0.0, scale=1.0),
                                reads=[b_ptr[half]], writes=[b_pt])
                    for dt in range(NDC):
                        pg = dt % 2
                        fns = []
                        for kc in range(NKC):
                            fns.append(lambda e, dt=dt, kc=kc, pg=pg: e.matmul(
                                out=p_o[pg][:],
                                lhsT=s_vt[:, kc * D + dt * 128: kc * D + (dt + 1) * 128],
                                rhs=s_pt[:, kc * 512:(kc + 1) * 512],
                                start=(kc == 0), stop=(kc == NKC - 1)))
                        te.group(fns, reads=[b_vt, b_pt], writes=[b_po[pg]])
                        sc.op(lambda e, dt=dt, qg=qg, pg=pg: e.activation(
                            out=s_at[:, dt * TOK + qg * 512: dt * TOK + (qg + 1) * 512],
                            in_=p_o[pg][:], func=AF.Identity, bias=0.0, scale=1.0),
                            reads=[b_po[pg]], writes=[b_at])

                P.barrier()

            if "at_scr" in taps:
                b_atdbg = Buf("at_dbg")
                for dc_ in range(NDC):
                    P.dma("stq", lambda e, dc_=dc_: e.dma_start(
                        out=d_at_dbg[dc_ * 128:(dc_ + 1) * 128, :],
                        in_=s_at[:, dc_ * TOK:(dc_ + 1) * TOK]),
                        reads=[b_at], writes=[b_atdbg], stream="atdbg")

            with contextlib.ExitStack() as dctx:
                s_woc = dctx.enter_context(nc.sbuf_tensor("wo_col", [128, 4 * NDC * 128], F32))
                b_woc = [Buf(f"woc{i}") for i in range(4)]
                s_wob = dctx.enter_context(nc.sbuf_tensor("wo_bf", [128, 4 * NDC * 128], BF16))
                b_wob = [Buf(f"wob{i}") for i in range(4)]
                s_xo = dctx.enter_context(nc.sbuf_tensor("xo_sb", [128, 4 * 512], F32))
                b_xo = [Buf(f"xo{i}") for i in range(4)]
                s_r1c = dctx.enter_context(nc.sbuf_tensor("r1c_sb", [128, 4 * 512], F32))
                b_r1c = [Buf(f"r1c{i}") for i in range(4)]
                s_r1b = dctx.enter_context(nc.sbuf_tensor("r1b_sb", [128, 4 * 512], BF16))
                b_r1b = [Buf(f"r1b{i}") for i in range(4)]
                s_r1q = dctx.enter_context(nc.sbuf_tensor("r1q_sb", [128, 4 * 512], BF16))
                b_r1q = [Buf(f"r1q{i}") for i in range(4)]
                s_l2t = dctx.enter_context(nc.sbuf_tensor("ln2_t", [128, 2 * 512], F32))
                b_l2t = Buf("ln2_t")
                p_w = [dctx.enter_context(nc.psum_tensor(f"pW{i}", [128, 512], F32)) for i in range(4)]
                b_pw = [Buf(f"pW{i}") for i in range(4)]
                p_t1 = [dctx.enter_context(nc.psum_tensor(f"pT1_{i}", [128, 512], F32)) for i in range(2)]
                p_t2 = [dctx.enter_context(nc.psum_tensor(f"pT2_{i}", [128, 512], F32)) for i in range(2)]
                b_pt1 = [Buf("pT1_0"), Buf("pT1_1")]
                b_pt2 = [Buf("pT2_0"), Buf("pT2_1")]

                wo_r = d_wo.rearrange("(c p) h -> p c h", p=128)
                woc_r = s_woc[:].rearrange("p (g c h) -> p g c h", g=4, c=NDC)

                stats_pend = []

                def d_stats(hc, tcc, sl):
                    te.group([lambda e, tcc=tcc, hc=hc, sl=sl: e.matmul(
                        out=p_t1[tcc][:], lhsT=s_ones[:],
                        rhs=s_r1b[:, sl * 512:(sl + 1) * 512],
                        start=(hc == 0), stop=(hc == NHC - 1))],
                        reads=[b_r1b[sl], b_ones], writes=[b_pt1[tcc]])
                    te.group([lambda e, tcc=tcc, hc=hc, sl=sl: e.matmul(
                        out=p_t2[tcc][:], lhsT=s_ones[:],
                        rhs=s_r1q[:, sl * 512:(sl + 1) * 512],
                        start=(hc == 0), stop=(hc == NHC - 1))],
                        reads=[b_r1q[sl], b_ones], writes=[b_pt2[tcc]])

                def d_wload(hc):
                    pg = hc % 4
                    P.dma("ldq",
                          lambda e, hc=hc, pg=pg: e.dma_start(
                              out=woc_r[:, pg], in_=wo_r[:, :, hc * 128:(hc + 1) * 128]),
                          writes=[b_woc[pg]], stream=f"wo{hc % 2}")
                    ve.op(lambda e, pg=pg: e.tensor_copy(
                        out=s_wob[:, pg * 1024:(pg + 1) * 1024],
                        in_=s_woc[:, pg * 1024:(pg + 1) * 1024]),
                        reads=[b_woc[pg]], writes=[b_wob[pg]])

                d_wload(0)
                d_wload(1)
                for hc in range(NHC):
                    pg = hc % 4
                    if hc + 2 < NHC:
                        d_wload(hc + 2)
                    for tcc in range(2):
                        sl = (hc * 2 + tcc) % 4
                        fns = []
                        for dc in range(NDC):
                            fns.append(lambda e, pg=pg, dc=dc, tcc=tcc, sl=sl: e.matmul(
                                out=p_w[sl][:],
                                lhsT=s_wob[:, pg * 1024 + dc * 128: pg * 1024 + (dc + 1) * 128],
                                rhs=s_at[:, dc * TOK + tcc * 512: dc * TOK + (tcc + 1) * 512],
                                start=(dc == 0), stop=(dc == NDC - 1)))
                        te.group(fns, reads=[b_wob[pg], b_at], writes=[b_pw[sl]])
                        P.dma("ldq",
                              lambda e, hc=hc, tcc=tcc, sl=sl: e.dma_start(
                                  out=s_xo[:, sl * 512:(sl + 1) * 512],
                                  in_=d_xT[hc * 128:(hc + 1) * 128, tcc * 512:(tcc + 1) * 512]),
                              writes=[b_xo[sl]], stream=f"xo{sl % 2}")
                        r1c = s_r1c[:, sl * 512:(sl + 1) * 512]
                        ve.op(lambda e, tcc=tcc, hc=hc, r1c=r1c, sl=sl: e.tensor_scalar(
                            out=r1c, in0=p_w[sl][:], scalar1=s_bo[:, hc:hc + 1],
                            scalar2=None, op0=ALU.add),
                            reads=[b_pw[sl], b_bo], writes=[b_r1c[sl]])
                        ve.op(lambda e, r1c=r1c, sl=sl: e.tensor_add(
                            out=r1c, in0=r1c, in1=s_xo[:, sl * 512:(sl + 1) * 512]),
                            reads=[b_xo[sl], b_r1c[sl]], writes=[b_r1c[sl]])
                        P.dma("stq",
                              lambda e, hc=hc, tcc=tcc, r1c=r1c: e.dma_start(
                                  out=d_r1[hc * 128:(hc + 1) * 128, tcc * 512:(tcc + 1) * 512],
                                  in_=r1c),
                              reads=[b_r1c[sl]], writes=[b_dr1], stream=f"r1st{tcc}")
                        ve.op(lambda e, r1c=r1c, sl=sl: e.tensor_copy(
                            out=s_r1b[:, sl * 512:(sl + 1) * 512], in_=r1c),
                            reads=[b_r1c[sl]], writes=[b_r1b[sl]])
                        sc.op(lambda e, r1c=r1c, sl=sl: e.activation(
                            out=s_r1q[:, sl * 512:(sl + 1) * 512], in_=r1c,
                            func=AF.Square, bias=0.0, scale=1.0),
                            reads=[b_r1c[sl]], writes=[b_r1q[sl]])
                        stats_pend.append((hc, tcc, sl))
                        if len(stats_pend) > 2:
                            d_stats(*stats_pend.pop(0))

                for _st_args in stats_pend:
                    d_stats(*_st_args)
                stats_pend = []

                for tcc in range(2):
                    msq = s_l2t[:, tcc * 512:(tcc + 1) * 512]
                    rst2 = s_ln2[:, tcc * 512:(tcc + 1) * 512]
                    mr2 = s_ln2[:, (2 + tcc) * 512:(3 + tcc) * 512]
                    sc.op(lambda e, tcc=tcc, msq=msq: e.activation(
                        out=msq, in_=p_t1[tcc][:], func=AF.Square, bias=0.0, scale=1.0 / H),
                        reads=[b_pt1[tcc]], writes=[b_l2t])
                    ve.op(lambda e, tcc=tcc, msq=msq: e.scalar_tensor_tensor(
                        out=msq, in0=p_t2[tcc][:], scalar=1.0 / H, in1=msq,
                        op0=ALU.mult, op1=ALU.subtract),
                        reads=[b_pt2[tcc], b_l2t], writes=[b_l2t])
                    sc.op(lambda e, msq=msq: e.activation(
                        out=msq, in_=msq, func=AF.Sqrt, bias=s_eps[:, 0:1], scale=1.0),
                        reads=[b_l2t, b_eps], writes=[b_l2t])
                    ve.op(lambda e, msq=msq, rst2=rst2: e.reciprocal(out=rst2, in_=msq),
                          reads=[b_l2t], writes=[b_ln2])
                    ve.op(lambda e, tcc=tcc, rst2=rst2, mr2=mr2: e.scalar_tensor_tensor(
                        out=mr2, in0=p_t1[tcc][:], scalar=1.0 / H, in1=rst2,
                        op0=ALU.mult, op1=ALU.mult),
                        reads=[b_pt1[tcc], b_ln2], writes=[b_ln2])

                P.barrier()

        with contextlib.ExitStack() as ectx:
            s_r1r = ectx.enter_context(nc.sbuf_tensor("r1r", [128, 4 * 512], F32))
            b_r1r = [Buf(f"r1r{i}") for i in range(4)]

            with contextlib.ExitStack() as exctx:
                s_xn2 = exctx.enter_context(nc.sbuf_tensor("xn2", [128, NHC * TOK], BF16))
                b_xn2t = [Buf("xn2t0"), Buf("xn2t1")]

                s_gq = exctx.enter_context(nc.sbuf_tensor("gq_sb", [128, 2 * H], U16))
                s_uq = exctx.enter_context(nc.sbuf_tensor("uq_sb", [128, 2 * H], U16))
                b_gq = [Buf("gq0"), Buf("gq1")]
                b_uq = [Buf("uq0"), Buf("uq1")]
                s_gsz = exctx.enter_context(nc.sbuf_tensor("gsz_sb", [128, 2 * 128], F32))
                b_gsz = [Buf("gsz0"), Buf("gsz1")]
                s_wg = exctx.enter_context(nc.sbuf_tensor("wg_deq", [128, H], BF16))
                s_wu = exctx.enter_context(nc.sbuf_tensor("wu_deq", [128, H], BF16))
                b_wgh = [Buf("wgA"), Buf("wgB")]
                b_wuh = [Buf("wuA"), Buf("wuB")]
                s_wgt = exctx.enter_context(nc.sbuf_tensor("wgT", [128, 2 * H], BF16))
                s_wut = exctx.enter_context(nc.sbuf_tensor("wuT", [128, 2 * H], BF16))
                b_wgt = [Buf("wgT0"), Buf("wgT1")]
                b_wut = [Buf("wuT0"), Buf("wuT1")]
                s_sg = exctx.enter_context(nc.sbuf_tensor("sg_sb", [128, TOK], F32))
                b_sg = Buf("sg_sb")
                s_su = exctx.enter_context(nc.sbuf_tensor("su_sb", [128, TOK], F32))
                b_su = Buf("su_sb")
                s_hac = exctx.enter_context(nc.sbuf_tensor("hact_sb", [128, 2 * TOK], BF16))
                b_hac = [Buf("hac0"), Buf("hac1")]
                p_g = [exctx.enter_context(nc.psum_tensor(f"pG{i}", [128, 512], F32)) for i in range(2)]
                p_u = [exctx.enter_context(nc.psum_tensor(f"pU{i}", [128, 512], F32)) for i in range(2)]
                b_pg = [Buf("pG0"), Buf("pG1")]
                b_pu = [Buf("pU0"), Buf("pU1")]
                p_tre = [exctx.enter_context(nc.psum_tensor(f"pTe{i}", [128, 1024], BF16)) for i in range(4)]
                b_ptre = [Buf(f"pTe{i}") for i in range(4)]

                def e_loads(ic):
                    pg = ic % 2
                    P.dma("ldq", lambda e, ic=ic, pg=pg: e.dma_start(
                        out=s_gq[:, pg * H:(pg + 1) * H],
                        in_=d_gq[ic * 128:(ic + 1) * 128, :]),
                        writes=[b_gq[pg]], stream=f"gq{pg}")
                    P.dma("ldq", lambda e, ic=ic, pg=pg: e.dma_start(
                        out=s_uq[:, pg * H:(pg + 1) * H],
                        in_=d_uq[ic * 128:(ic + 1) * 128, :]),
                        writes=[b_uq[pg]], stream=f"uq{pg}")
                    P.dma("ldq", lambda e, ic=ic, pg=pg: e.dma_start(
                        out=s_gsz[:, pg * 128:(pg + 1) * 128],
                        in_=d_gusz[ic * 128:(ic + 1) * 128, :]),
                        writes=[b_gsz[pg]], stream=f"gsz{pg}")

                def e_dequant(ic):
                    pg = ic % 2
                    for half in range(2):
                        fns = []
                        for g in range(half * 16, half * 16 + 16):
                            fns.append(lambda e, pg=pg, g=g: e.tensor_scalar(
                                out=s_wg[:, g * 128:(g + 1) * 128],
                                in0=s_gq[:, pg * H + g * 128: pg * H + (g + 1) * 128],
                                scalar1=s_gsz[:, pg * 128 + 32 + g: pg * 128 + 33 + g],
                                scalar2=s_gsz[:, pg * 128 + g: pg * 128 + g + 1],
                                op0=ALU.subtract, op1=ALU.mult))
                        ve.group(fns, reads=[b_gq[pg], b_gsz[pg]], writes=[b_wgh[half]])
                    for half in range(2):
                        fns = []
                        for g in range(half * 16, half * 16 + 16):
                            fns.append(lambda e, pg=pg, g=g: e.tensor_scalar(
                                out=s_wu[:, g * 128:(g + 1) * 128],
                                in0=s_uq[:, pg * H + g * 128: pg * H + (g + 1) * 128],
                                scalar1=s_gsz[:, pg * 128 + 96 + g: pg * 128 + 97 + g],
                                scalar2=s_gsz[:, pg * 128 + 64 + g: pg * 128 + 65 + g],
                                op0=ALU.subtract, op1=ALU.mult))
                        ve.group(fns, reads=[b_uq[pg], b_gsz[pg]], writes=[b_wuh[half]])

                def e_trans_grp(ic, wsel, grp):
                    pg = ic % 2
                    s_wd, b_wd = (s_wg, b_wgh[grp // 2]) if wsel == 0 else (s_wu, b_wuh[grp // 2])
                    s_wt, b_wt = (s_wgt, b_wgt) if wsel == 0 else (s_wut, b_wut)
                    tpg = (wsel * 4 + grp) % 4
                    fns = []
                    for j in range(8):
                        g = grp * 8 + j
                        fns.append(lambda e, g=g, j=j, tpg=tpg, s_wd=s_wd: e.transpose(
                            out=p_tre[tpg][:, j * 128:(j + 1) * 128],
                            in_=s_wd[:, g * 128:(g + 1) * 128],
                            identity=s_id[:]))
                    te.group(fns, reads=[b_wd, b_id], writes=[b_ptre[tpg]])
                    sc.op(lambda e, grp=grp, tpg=tpg, pg=pg, s_wt=s_wt: e.activation(
                        out=s_wt[:, pg * H + grp * 1024: pg * H + (grp + 1) * 1024],
                        in_=p_tre[tpg][:], func=AF.Identity, bias=0.0, scale=1.0),
                        reads=[b_ptre[tpg]], writes=[b_wt[pg]])

                def e_mms(ic, interleave):
                    pg = ic % 2
                    step = 0
                    for tcc in range(2):
                        for w in range(2):
                            pp, bb, wt, bwt = ((p_g, b_pg, s_wgt, b_wgt) if w == 0
                                               else (p_u, b_pu, s_wut, b_wut))
                            fns = []
                            for hcc in range(NHC):
                                fns.append(lambda e, pg=pg, hcc=hcc, tcc=tcc, pp=pp, wt=wt: e.matmul(
                                    out=pp[tcc][:],
                                    lhsT=wt[:, pg * H + hcc * 128: pg * H + (hcc + 1) * 128],
                                    rhs=s_xn2[:, hcc * TOK + tcc * 512: hcc * TOK + (tcc + 1) * 512],
                                    start=(hcc == 0), stop=(hcc == NHC - 1)))
                            te.group(fns, reads=[bwt[pg], b_xn2t[tcc]], writes=[bb[tcc]])
                            if step == 0:
                                for (iw, ig) in interleave:
                                    e_trans_grp(ic + 1, iw, ig)
                            step += 1
                        sgc = s_sg[:, tcc * 512:(tcc + 1) * 512]
                        suc = s_su[:, tcc * 512:(tcc + 1) * 512]
                        sc.op(lambda e, tcc=tcc, sgc=sgc: e.activation(
                            out=sgc, in_=p_g[tcc][:], func=AF.Silu, bias=0.0, scale=1.0),
                            reads=[b_pg[tcc]], writes=[b_sg])
                        sc.op(lambda e, tcc=tcc, suc=suc: e.activation(
                            out=suc, in_=p_u[tcc][:], func=AF.Identity, bias=0.0, scale=1.0),
                            reads=[b_pu[tcc]], writes=[b_su])
                        gp.op(lambda e, tcc=tcc, pg=pg, sgc=sgc, suc=suc: e.tensor_mul(
                            out=s_hac[:, pg * TOK + tcc * 512: pg * TOK + (tcc + 1) * 512],
                            in0=suc, in1=sgc),
                            reads=[b_sg, b_su], writes=[b_hac[pg]])
                    P.dma("stq", lambda e, ic=ic, pg=pg: e.dma_start(
                        out=d_ha[ic * 128:(ic + 1) * 128, :],
                        in_=s_hac[:, pg * TOK:(pg + 1) * TOK]),
                        reads=[b_hac[pg]], writes=[b_dha], stream=f"hast{pg}")

                ALLGRPS = [(w_, g_) for w_ in range(2) for g_ in range(4)]
                e_loads(0)
                e_dequant(0)
                for (w_, g_) in ALLGRPS:
                    e_trans_grp(0, w_, g_)

                for tcc in range(2):
                    for hc in range(NHC):
                        idx = tcc * NHC + hc
                        pg = idx % 4
                        eng_ = ve if (idx % 3) else gp
                        rr = s_r1r[:, pg * 512:(pg + 1) * 512]
                        P.dma("ldq",
                              lambda e, hc=hc, tcc=tcc, rr=rr: e.dma_start(
                                  out=rr, in_=d_r1[hc * 128:(hc + 1) * 128,
                                                  tcc * 512:(tcc + 1) * 512]),
                              reads=[b_dr1], writes=[b_r1r[pg]], stream=f"r1ld{pg % 2}")
                        rst2 = s_ln2[:, tcc * 512:(tcc + 1) * 512]
                        mr2 = s_ln2[:, (2 + tcc) * 512:(3 + tcc) * 512]
                        eng_.op(lambda e, rr=rr, rst2=rst2: e.tensor_mul(
                            out=rr, in0=rr, in1=rst2),
                            reads=[b_ln2, b_r1r[pg]], writes=[b_r1r[pg]])
                        eng_.op(lambda e, rr=rr, mr2=mr2, hc=hc, tcc=tcc: e.tensor_sub(
                            out=s_xn2[:, hc * TOK + tcc * 512: hc * TOK + (tcc + 1) * 512],
                            in0=rr, in1=mr2),
                            reads=[b_r1r[pg], b_ln2], writes=[b_xn2t[tcc]])


                for ic in range(NIC):
                    if ic + 1 < NIC:
                        e_loads(ic + 1)
                        e_dequant(ic + 1)
                        e_mms(ic, ALLGRPS)
                    else:
                        e_mms(ic, [])

                P.barrier()

            with contextlib.ExitStack() as fctx:
                s_ha = fctx.enter_context(nc.sbuf_tensor("ha_sb", [128, NIC * 512], BF16))
                b_ha = Buf("ha_sb")
                s_dqs = fctx.enter_context(nc.sbuf_tensor("dq_sb", [128, 2 * 2816], U16))
                b_dqs = [Buf("dqs0"), Buf("dqs1")]
                FQ_START = (0, 22, 43, 65)
                FQ_CNT = (22, 21, 22, 21)
                s_dsz = fctx.enter_context(nc.sbuf_tensor("dsz_sb", [128, 2 * 172], F32))
                b_dsz = [Buf("dsz0"), Buf("dsz1")]
                s_wdd = fctx.enter_context(nc.sbuf_tensor("wd_deq", [128, FI], BF16))
                b_wddq = [Buf(f"wdd{i}") for i in range(4)]
                s_wdt = fctx.enter_context(nc.sbuf_tensor("wdT", [128, 2 * FI], BF16))
                b_wdt = [Buf("wdT0"), Buf("wdT1")]
                s_r1f = fctx.enter_context(nc.sbuf_tensor("r1f", [128, 2 * 512], F32))
                b_r1f = [Buf("r1f0"), Buf("r1f1")]
                s_yo = fctx.enter_context(nc.sbuf_tensor("y_sb", [128, 2 * 512], F32))
                b_yo = [Buf("yo0"), Buf("yo1")]
                s_yt = fctx.enter_context(nc.sbuf_tensor("yt_sb", [128, 2 * 512], F32))
                b_yt = [Buf("yt0"), Buf("yt1")]
                p_d = [fctx.enter_context(nc.psum_tensor(f"pD{i}", [128, 512], F32)) for i in range(2)]
                b_pd = [Buf("pD0"), Buf("pD1")]
                p_trf = [fctx.enter_context(nc.psum_tensor(f"pTf{i}", [128, 1024], BF16)) for i in range(4)]
                b_ptrf = [Buf(f"pTf{i}") for i in range(4)]

                FTR_SCHED = {1: (0, 1), 2: (2, 3), 3: (4, 5), 4: (6, 7),
                             5: (8, 9), 6: (10,)}
                ha_r = d_ha.rearrange("(c p) t -> p c t", p=128)
                hasb_r = s_ha[:].rearrange("p (c t) -> p c t", c=NIC)

                def f1_loads(hc):
                    pg = hc % 2
                    P.dma("ldq", lambda e, hc=hc, pg=pg: e.dma_start(
                        out=s_dsz[:, pg * 172:(pg + 1) * 172],
                        in_=d_dsdz[hc * 128:(hc + 1) * 128, :]),
                        writes=[b_dsz[pg]], stream=f"dsz{pg}")

                def f1_deq(hc, fq):
                    pg = hc % 2
                    qb = fq % 2
                    c0, cn = FQ_START[fq] * 128, FQ_CNT[fq] * 128
                    P.dma("ldq", lambda e, hc=hc, qb=qb, c0=c0, cn=cn: e.dma_start(
                        out=s_dqs[:, qb * 2816: qb * 2816 + cn],
                        in_=d_dq[hc * 128:(hc + 1) * 128, c0:c0 + cn]),
                        writes=[b_dqs[qb]], stream=f"dq{qb}")
                    fns = []
                    for j in range(FQ_CNT[fq]):
                        ic = FQ_START[fq] + j
                        fns.append(lambda e, pg=pg, qb=qb, j=j, ic=ic: e.tensor_scalar(
                            out=s_wdd[:, ic * 128:(ic + 1) * 128],
                            in0=s_dqs[:, qb * 2816 + j * 128: qb * 2816 + (j + 1) * 128],
                            scalar1=s_dsz[:, pg * 172 + 86 + ic: pg * 172 + 87 + ic],
                            scalar2=s_dsz[:, pg * 172 + ic: pg * 172 + ic + 1],
                            op0=ALU.subtract, op1=ALU.mult))
                    ve.group(fns, reads=[b_dqs[qb], b_dsz[pg]], writes=[b_wddq[fq]])

                def f1_trans_grp(hc, grp):
                    wpg = hc % 2
                    tpg = grp % 4
                    cnt = min(8, NIC - grp * 8)
                    lo = grp * 8
                    hi = lo + cnt - 1
                    qr = [q for q in range(4)
                          if not (FQ_START[q] + FQ_CNT[q] - 1 < lo or FQ_START[q] > hi)]
                    fns = []
                    for j in range(cnt):
                        fns.append(lambda e, grp=grp, j=j, tpg=tpg: e.transpose(
                            out=p_trf[tpg][:, j * 128:(j + 1) * 128],
                            in_=s_wdd[:, (grp * 8 + j) * 128:(grp * 8 + j + 1) * 128],
                            identity=s_id[:]))
                    te.group(fns, reads=[b_wddq[q] for q in qr] + [b_id],
                             writes=[b_ptrf[tpg]])
                    sc.op(lambda e, wpg=wpg, grp=grp, cnt=cnt, tpg=tpg: e.activation(
                        out=s_wdt[:, wpg * FI + grp * 1024: wpg * FI + grp * 1024 + cnt * 128],
                        in_=p_trf[tpg][:, 0:cnt * 128],
                        func=AF.Identity, bias=0.0, scale=1.0),
                        reads=[b_ptrf[tpg]], writes=[b_wdt[wpg]])

                def f_mms(hc, tc0, trs_hc=None, deq_hc=None):
                    pg = hc % 2
                    wpg = hc % 2
                    for sub in range(11):
                        cnt = min(8, NIC - sub * 8)
                        fns = []
                        for j in range(cnt):
                            ic = sub * 8 + j
                            fns.append(lambda e, ic=ic, pg=pg, wpg=wpg, sub=sub: e.matmul(
                                out=p_d[pg][:],
                                lhsT=s_wdt[:, wpg * FI + ic * 128: wpg * FI + (ic + 1) * 128],
                                rhs=s_ha[:, ic * 512:(ic + 1) * 512],
                                start=(ic == 0), stop=(ic == NIC - 1)))
                        te.group(fns, reads=[b_wdt[wpg], b_ha], writes=[b_pd[pg]])
                        if trs_hc is not None:
                            for g_ in FTR_SCHED.get(sub, ()):
                                f1_trans_grp(trs_hc, g_)
                        if deq_hc is not None and sub in (2, 4, 6, 8):
                            f1_deq(deq_hc, (sub - 2) // 2)

                    P.dma("ldq", lambda e, hc=hc, tc0=tc0, pg=pg: e.dma_start(
                        out=s_r1f[:, pg * 512:(pg + 1) * 512],
                        in_=d_r1[hc * 128:(hc + 1) * 128, tc0:tc0 + 512]),
                        reads=[b_dr1], writes=[b_r1f[pg]], stream=f"r1f{pg}")
                    sc.op(lambda e, pg=pg: e.activation(
                        out=s_yt[:, pg * 512:(pg + 1) * 512], in_=p_d[pg][:],
                        func=AF.Identity, bias=0.0, scale=1.0),
                        reads=[b_pd[pg]], writes=[b_yt[pg]])
                    gp.op(lambda e, pg=pg: e.tensor_add(
                        out=s_yo[:, pg * 512:(pg + 1) * 512],
                        in0=s_yt[:, pg * 512:(pg + 1) * 512],
                        in1=s_r1f[:, pg * 512:(pg + 1) * 512]),
                        reads=[b_yt[pg], b_r1f[pg]], writes=[b_yo[pg]])
                    P.dma("stq", lambda e, hc=hc, tc0=tc0, pg=pg: e.dma_start(
                        out=d_y[hc * 128:(hc + 1) * 128, tc0:tc0 + 512],
                        in_=s_yo[:, pg * 512:(pg + 1) * 512]),
                        reads=[b_yo[pg]], writes=[b_dy], stream=f"yst{pg}")

                P.dma("ldq", lambda e: e.dma_start(
                    out=hasb_r, in_=ha_r[:, :, 0:512]),
                    reads=[b_dha], writes=[b_ha], stream="hald")
                _PRO_DEQ = {2: 0, 5: 1, 8: 2, 10: 3}

                def f_prologue(hc0):
                    f1_loads(hc0)
                    for fq in range(4):
                        f1_deq(hc0, fq)
                    f1_loads(hc0 + 1)
                    for grp in range(11):
                        f1_trans_grp(hc0, grp)
                        if grp in _PRO_DEQ:
                            f1_deq(hc0 + 1, _PRO_DEQ[grp])

                f_prologue(0)
                for hc in range(NHC):
                    if hc + 2 < NHC:
                        f1_loads(hc + 2)
                    f_mms(hc, 0,
                          trs_hc=hc + 1 if hc + 1 < NHC else None,
                          deq_hc=hc + 2 if hc + 2 < NHC else None)
                P.barrier()
                P.dma("ldq", lambda e: e.dma_start(
                    out=hasb_r, in_=ha_r[:, :, 512:1024]),
                    reads=[b_dha], writes=[b_ha], stream="hald")
                f_prologue(0)
                for hc in range(NHC):
                    if hc + 2 < NHC:
                        f1_loads(hc + 2)
                    f_mms(hc, 512,
                          trs_hc=hc + 1 if hc + 1 < NHC else None,
                          deq_hc=hc + 2 if hc + 2 < NHC else None)

                for (qn, _st), s_ in list(P.dsems.items()):
                    if qn in ("stq", "scq"):
                        gp._wait(s_, s_.c)
                P.barrier()

        scr_map = {"q_scr": d_q, "k_scr": d_k, "v_scr": d_v,
                   "r1_scr": d_r1, "hact_scr": d_ha, "at_scr": d_at_dbg}
        for name in taps:
            dsrc = scr_map[name]
            d_tap = nc.dram_tensor("tap_" + name, list(dsrc.shape), F32,
                                   kind="ExternalOutput").ap()
            P.dma("stq", lambda e, d_tap=d_tap, dsrc=dsrc: e.dma_start(
                out=d_tap, in_=dsrc), stream="taps")
        if taps:
            for (qn, _st), s_ in list(P.dsems.items()):
                if qn == "stq":
                    gp._wait(s_, s_.c)

        with nc.Block() as block:
            @block.sync
            def _(e):
                for f in P.e["sync"].items:
                    f(e)

            @block.vector
            def _(e):
                for f in P.e["vector"].items:
                    f(e)

            @block.scalar
            def _(e):
                for f in P.e["scalar"].items:
                    f(e)

            @block.tensor
            def _(e):
                for f in P.e["tensor"].items:
                    f(e)

            @block.gpsimd
            def _(e):
                for f in P.e["gpsimd"].items:
                    f(e)

    return nc


def _lowu16(a):
    a = np.ascontiguousarray(np.asarray(a, dtype="<i4"))
    return np.ascontiguousarray(a.view(np.uint16)[:, 0::2])


def _prep_inputs(inputs):
    x = np.asarray(inputs["L0_attn_x"], dtype=np.float32)
    bqkv = np.asarray(inputs["L0_attn_bqkv"], dtype=np.float32)
    bo = np.asarray(inputs["L0_attn_bo"], dtype=np.float32)
    com = {
        "wqkv": np.ascontiguousarray(np.asarray(inputs["L0_attn_Wqkv"], dtype=np.float32)),
        "bqkv_r": np.ascontiguousarray(bqkv.reshape(24, 128).T),
        "wo": np.ascontiguousarray(np.asarray(inputs["L0_attn_Wo"], dtype=np.float32)),
        "bo_r": np.ascontiguousarray(bo.reshape(32, 128).T),
        "gq": _lowu16(inputs["L0_ffn_gate_qweight"]),
        "uq": _lowu16(inputs["L0_ffn_up_qweight"]),
        "dq": _lowu16(inputs["L0_ffn_down_qweight"]),
        "gusz": np.ascontiguousarray(np.concatenate([
            np.asarray(inputs["L0_ffn_gate_scales"], dtype=np.float32),
            np.asarray(inputs["L0_ffn_gate_zeros"], dtype=np.float32),
            np.asarray(inputs["L0_ffn_up_scales"], dtype=np.float32),
            np.asarray(inputs["L0_ffn_up_zeros"], dtype=np.float32)], axis=1)),
        "dsdz": np.ascontiguousarray(np.concatenate([
            np.asarray(inputs["L0_ffn_down_scales"], dtype=np.float32),
            np.asarray(inputs["L0_ffn_down_zeros"], dtype=np.float32)], axis=1)),
        "ident": np.eye(128, dtype=ml_dtypes.bfloat16),
    }
    in_maps = []
    for c in range(8):
        b, hh = c // 2, c % 2
        own = x[b, hh * TOK:(hh + 1) * TOK].T
        other = x[b, (1 - hh) * TOK:(2 - hh) * TOK].T
        m = dict(com)
        m["xT"] = np.ascontiguousarray(np.concatenate([own, other], axis=1))
        in_maps.append(m)
    return in_maps


def kernel(**inputs):
    in_maps = _prep_inputs(inputs)
    nc = build_nc()
    res = run_bass_kernel_spmd(nc, in_maps, core_ids=list(range(8)))
    out = np.empty((4, 2048, 4096), dtype=np.float32)
    for c in range(8):
        b, hh = c // 2, c % 2
        y = np.asarray(res.results[c]["y"], dtype=np.float32)
        out[b, hh * TOK:(hh + 1) * TOK, :] = y.T
    return out

